# version 03
# baseline (speedup 1.0000x reference)
import math
import numpy as np
import ml_dtypes
from contextlib import ExitStack

import concourse.bass as bass
import concourse.mybir as mybir
from concourse.bass_utils import run_bass_kernel_spmd

F32 = mybir.dt.float32
BF16 = mybir.dt.bfloat16
AF = mybir.ActivationFunctionType
ALU = mybir.AluOpType
AXX = mybir.AxisListType.X

T = 2048
TP = T + 1
C = 1024
NCH = 8
LAM_INIT = 0.8 - 0.6 * math.exp(-0.3 * 12)
ATTB_SCALE = 32.0 ** -0.5
HID = 2730
DEDUP_WAITS = True
DBG = {"keep": False, "names": {}}


class Em:
    NDMA = 8
    PHYS = {"T": "tensor", "V": "vector", "S": "scalar", "G": "gpsimd"}
    for _i in range(NDMA):
        PHYS[f"GD{_i}"] = "gpsimd"
        PHYS[f"SD{_i}"] = "sync"
    ORDER = ["tensor", "vector", "scalar", "gpsimd", "sync"]
    DMA = tuple(f"{q}{i}" for q in ("GD", "SD") for i in range(8))

    def __init__(self, nc, ctx):
        self.nc = nc
        self.sems = {v: ctx.enter_context(nc.semaphore("sem" + v)) for v in self.PHYS}
        self.ops = {p: [] for p in self.ORDER}
        self.seq = {v: 0 for v in self.PHYS}
        self.pending = {v: {} for v in self.PHYS}
        self.known = {v: {w: 0 for w in self.PHYS} for v in self.PHYS}
        self.lastw = {}
        self.readers = {}
        self.targets = {v: set() for v in self.PHYS}
        self.rr = {"GD": 0, "SD": 0}

    @staticmethod
    def _conf(k1, k2):
        n1, n2 = str(k1[0]), str(k2[0])
        if PSUM_ONEKEY == 1 and n1.startswith("ps") and n2.startswith("ps"):
            return True
        if PSUM_ONEKEY == 2:
            b1 = n1 == "psb" or n1 == "psbL" or n1 == "psbg"
            b2 = n2 == "psb" or n2 == "psbL" or n2 == "psbg"
            a1 = n1.startswith("ps") and not b1
            a2 = n2.startswith("ps") and not b2
            if (b1 and a2) or (b2 and a1):
                return True
        return k1[0] == k2[0] and (k1[1] is None or k2[1] is None or k1[1] == k2[1])

    def op(self, v, fn, reads=(), writes=()):
        if v in ("GD", "SD"):
            q = v
            v = f"{q}{self.rr[q] % self.NDMA}"
            self.rr[q] += 1
        deps = {}

        def dep(pv, pseq):
            if pv == v and v == "T":
                return
            deps[pv] = max(deps.get(pv, 0), pseq)

        for key in reads:
            for k2, (pv, pseq) in self.lastw.items():
                if self._conf(key, k2):
                    dep(pv, pseq)
        for key in writes:
            for k2, (pv, pseq) in self.lastw.items():
                if self._conf(key, k2):
                    dep(pv, pseq)
            for k2, rd in self.readers.items():
                if self._conf(key, k2):
                    for pv, pseq in rd.items():
                        dep(pv, pseq)
        for pv, pseq in self.pending[v].items():
            deps[pv] = max(deps.get(pv, 0), pseq)
        self.pending[v] = {}
        if v in self.DMA and self.seq[v] > 0:
            deps[v] = max(deps.get(v, 0), self.seq[v])
        wl = []
        for pv, pseq in deps.items():
            if pv == v:
                wl.append((pv, pseq))
                self.targets[pv].add(pseq)
            else:
                if DEDUP_WAITS and pseq <= self.known[v][pv]:
                    continue
                wl.append((pv, pseq))
                self.targets[pv].add(pseq)
                self.known[v][pv] = max(self.known[v][pv], pseq)
        step = 16 if v in self.DMA else 1
        self.seq[v] += step
        myseq = self.seq[v]
        for key in writes:
            self.lastw[key] = (v, myseq)
            self.readers.pop(key, None)
        for key in reads:
            self.readers.setdefault(key, {})[v] = myseq
        self.ops[self.PHYS[v]].append((wl, v, myseq, fn))

    def barrier(self):
        for v in self.PHYS:
            for w in self.PHYS:
                if w != v and self.seq[w] > (self.known[v][w] if DEDUP_WAITS else 0):
                    self.pending[v][w] = max(self.pending[v].get(w, 0), self.seq[w])
                    self.targets[w].add(self.seq[w])
        self.lastw.clear()
        self.readers.clear()

    def render(self, block):
        incval = {}
        for v in self.PHYS:
            if v not in self.DMA:
                incval[v] = {s: i + 1 for i, s in enumerate(sorted(self.targets[v]))}

        def val(pv, pseq):
            if pv in self.DMA:
                return pseq
            m = incval[pv]
            if pseq in m:
                return m[pseq]
            return m[min(s for s in m if s >= pseq)]

        def make_body(phys, oplist, is_sync):
            def body(eng):
                for wl, v, myseq, fn in oplist:
                    for pv, pseq in wl:
                        eng.wait_ge(self.sems[pv], val(pv, pseq))
                    inst = fn(eng)
                    if v in self.DMA:
                        inst.then_inc(self.sems[v], 16)
                    elif myseq in incval[v]:
                        inst.then_inc(self.sems[v], 1)
                if is_sync:
                    for v in self.DMA:
                        if self.seq[v] > 0:
                            eng.wait_ge(self.sems[v], self.seq[v])
            return body

        for phys in self.ORDER:
            oplist = self.ops[phys]
            if not oplist and phys != "sync":
                continue
            getattr(block, phys)(make_body(phys, oplist, phys == "sync"))


PSUM_ONEKEY = False

def pk(b, sub=None):
    if PSUM_ONEKEY:
        return ("psALL", None)
    return (f"ps{b}", sub)



def _host_tables():
    t = {}
    t["ident_bf"] = np.eye(128).astype(ml_dtypes.bfloat16)
    t["ident_f32"] = np.eye(128, dtype=np.float32)
    half = np.arange(0, 64, 2, dtype=np.float32)
    inv_freq = (1.0 / 10000.0 ** (half / 64.0)).astype(np.float32)
    tt = np.arange(T, dtype=np.float32)
    freqs = np.concatenate([tt[:, None] * inv_freq[None, :]] * 2, axis=1)
    base = ((half + 0.4 * 64) / (1.4 * 64)).astype(np.float32)
    power = (tt - T // 2) / 256.0
    scale = np.concatenate([base[None, :] ** power[:, None]] * 2, axis=1)
    cos, sin = np.cos(freqs), np.sin(freqs)
    t["rot_csq"] = np.concatenate([(cos * scale).T] * 2, 0).astype(ml_dtypes.bfloat16)
    t["rot_csk"] = np.concatenate([(cos / scale).T] * 2, 0).astype(ml_dtypes.bfloat16)
    t["rot_ssq"] = np.concatenate([(sin * scale).T] * 2, 0).astype(ml_dtypes.bfloat16)
    t["rot_ssk"] = np.concatenate([(sin / scale).T] * 2, 0).astype(ml_dtypes.bfloat16)
    rp = np.zeros((128, 128), np.float32)
    for blk in range(2):
        b = 64 * blk
        for d in range(32):
            rp[b + d + 32, b + d] = -1.0
            rp[b + d, b + d + 32] = 1.0
    t["rotp"] = rp.astype(ml_dtypes.bfloat16)
    kk = np.arange(128)[:, None]
    cc = np.arange(1408)[None, :]
    allowed = (cc >= kk + 384) & (cc <= kk + 896)
    t["win_add"] = np.where(allowed, 0.0, -30000.0).astype(ml_dtypes.bfloat16)
    t["win_mul"] = allowed.astype(ml_dtypes.bfloat16)
    cc2 = np.arange(896)[None, :]
    t["tri"] = (cc2 >= kk + 384).astype(ml_dtypes.bfloat16)
    e2 = np.zeros((2, 128), np.float32)
    e2[0, 0:64] = 1.0
    e2[1, 64:128] = 1.0
    t["e2f"] = e2
    t["e2b"] = np.ascontiguousarray(e2.T).astype(ml_dtypes.bfloat16)
    return t

_TBL = None

def host_tables():
    global _TBL
    if _TBL is None:
        _TBL = _host_tables()
    return _TBL



class Prog:
    def __init__(self, nc, ctx, em):
        self.nc = nc
        self.ctx = ctx
        self.em = em
        self.hbm = {}
        self._stacks = []

    _uid = 0

    def _unm(self, name):
        Prog._uid += 1
        return f"{name}_u{Prog._uid}"

    def sb(self, name, shape, dt, side="left"):
        return self.ctx.enter_context(
            self.nc.sbuf_tensor(self._unm(name), shape, dt, side=side, unsafe_allow_realloc=True))

    def push(self):
        self._stacks.append(ExitStack())

    def pop(self):
        self._stacks.pop().close()
        self.em.barrier()

    def sbr(self, name, shape, dt):
        return self._stacks[-1].enter_context(
            self.nc.sbuf_tensor(self._unm(name), shape, dt, side="right", unsafe_allow_realloc=True))

    def bank(self, b, w=512, p0=0, p=128, off=0):
        return self.ps[p0:p0 + p, 512 * b + off:512 * b + off + w]


def dsl(j, w=512):
    return slice(1 + w * j, 1 + w * j + w)



def ln_half(P, g, rows, half, src, tag):
    em = P.em
    arow, brow, srow = rows
    sqt = g[f"{tag}sqt"]
    for c in range(NCH):
        for j in range(2):
            rk, ap = src(c, j)
            st, sp = (c == 0), (c == NCH - 1)
            i = (2 * c + j) % 2
            em.op("S", lambda e, ap=ap, i=i: e.activation(
                out=sqt[i][:], in_=ap, func=AF.Square),
                reads=rk, writes=[(f"{tag}sqt", i)])
            em.op("T", lambda e, ap=ap, j=j, st=st, sp=sp: e.matmul(
                P.bank(3 + j, p0=0, p=1), g["ones_col_b"][:], ap, start=st, stop=sp),
                reads=rk + [("ones_col_b", None)], writes=[pk(3 + j, "r0")])
            em.op("T", lambda e, j=j, i=i, st=st, sp=sp: e.matmul(
                P.bank(5 + j, p0=0, p=1), g["ones_col_b"][:], sqt[i][:],
                start=st, stop=sp),
                reads=[(f"{tag}sqt", i), ("ones_col_b", None)], writes=[pk(5 + j, "r1")])
    for j in range(2):
        sl = slice(512 * j, 512 * j + 512)
        em.op("V", lambda e, j=j, sl=sl: e.tensor_scalar(
            out=brow[0:1, sl], in0=P.bank(3 + j, p0=0, p=1), scalar1=1.0 / C,
            scalar2=None, op0=ALU.mult),
            reads=[pk(3 + j, "r0")], writes=[("lnrows", ("m", j))])
        em.op("V", lambda e, sl=sl: e.tensor_tensor(
            out=srow[0:1, sl], in0=brow[0:1, sl], in1=brow[0:1, sl], op=ALU.mult),
            reads=[("lnrows", ("m", j))], writes=[("lnrows", ("s", j))])
        em.op("V", lambda e, j=j, sl=sl: e.scalar_tensor_tensor(
            out=srow[0:1, sl], in0=P.bank(5 + j, p0=0, p=1), scalar=1.0 / C,
            in1=srow[0:1, sl], op0=ALU.mult, op1=ALU.subtract),
            reads=[pk(5 + j, "r1"), ("lnrows", ("s", j))], writes=[("lnrows", ("s", j))])
        em.op("S", lambda e, sl=sl: e.activation(
            out=srow[0:1, sl], in_=srow[0:1, sl], func=AF.Sqrt,
            bias=g["epsr"][0:1, :], scale=1.0),
            reads=[("lnrows", ("s", j)), ("epsr", None)],
            writes=[("lnrows", ("s", j))])
        em.op("V", lambda e, sl=sl: e.reciprocal(arow[0:1, sl], srow[0:1, sl]),
            reads=[("lnrows", ("s", j))], writes=[("lnrows", ("a", j))])
        em.op("V", lambda e, sl=sl: e.scalar_tensor_tensor(
            out=brow[0:1, sl], in0=brow[0:1, sl], scalar=-1.0, in1=arow[0:1, sl],
            op0=ALU.mult, op1=ALU.mult),
            reads=[("lnrows", ("a", j)), ("lnrows", ("m", j))],
            writes=[("lnrows", ("b", j))])


def ln_bcast(P, g, rows, j, bR, bB):
    em = P.em
    arow, brow, srow = rows
    sl = slice(512 * j, 512 * j + 512)
    em.op("T", lambda e: e.matmul(
        P.bank(bR), g["ones_row_f"][0:1, :], arow[0:1, sl], start=True, stop=True),
        reads=[("lnrows", ("a", j)), ("ones_row_f", None)], writes=[pk(bR)])
    em.op("T", lambda e: e.matmul(
        P.bank(bB), g["ones_row_f"][0:1, :], brow[0:1, sl], start=True, stop=True),
        reads=[("lnrows", ("b", j)), ("ones_row_f", None)], writes=[pk(bB)])



def p0_load(P, x, g):
    em, H = P.em, P.hbm
    P.push()
    idf = P.sbr("idf0", [128, 128], F32)
    stage = [P.sbr(f"stg{i}", [128, C], F32) for i in range(2)]
    for nm in ("ident_bf", "rotp"):
        em.op("SD", lambda e, nm=nm: e.dma_start(out=g[nm][:], in_=H[nm]),
              writes=[(nm, None)])
    em.op("SD", lambda e: e.dma_start(out=idf[:], in_=H["ident_f32"]),
          writes=[("idf0", None)])
    em.op("V", lambda e: e.memset(g["ones_row_f"][:], 1.0), writes=[("ones_row_f", None)])
    em.op("V", lambda e: e.memset(g["ones_row_b"][:], 1.0), writes=[("ones_row_b", None)])
    em.op("V", lambda e: e.memset(g["ones_col_b"][:], 1.0), writes=[("ones_col_b", None)])
    em.op("V", lambda e: e.memset(g["epsr"][:], 1e-5), writes=[("epsr", None)])
    em.op("SD", lambda e: e.dma_start(out=g["e2f"][:], in_=H["e2f"]),
          writes=[("e2f", None)])
    em.op("SD", lambda e: e.dma_start(out=g["e2b"][:], in_=H["e2b"]),
          writes=[("e2b", None)])
    for c in range(NCH):
        em.op("V", lambda e, c=c: e.memset(x[c][:, 0:1], 0.0), writes=[("x", (c, "p"))])
    if DBG.get("skip_p0"):
        for c in range(NCH):
            em.op("V", lambda e, c=c: e.memset(x[c][:, 1:TP], 0.25),
                  writes=[("x", (c, None))])
        P.pop()
        return
    for tt in range(16):
        st = stage[tt % 2]
        em.op("SD", lambda e, tt=tt, st=st: e.dma_start(
            out=st[:], in_=H["x_in"][128 * tt:128 * tt + 128, :]),
            writes=[("stg", tt % 2)])
        for c in range(NCH):
            bk = c % 3
            em.op("T", lambda e, st=st, c=c, bk=bk: e.transpose(
                P.bank(bk, 128), st[:, 128 * c:128 * c + 128], idf[:]),
                reads=[("stg", tt % 2), ("idf0", None)], writes=[pk(bk)])
            em.op("V", lambda e, c=c, tt=tt, bk=bk: e.tensor_copy(
                x[c][:, 1 + 128 * tt:1 + 128 * tt + 128], P.bank(bk, 128)),
                reads=[pk(bk)], writes=[("x", (c, tt // 4))])
    P.pop()


def p5_store(P, x, g):
    em, H = P.em, P.hbm
    if DBG.get("skip_p5"):
        for tt in range(16):
            em.op("SD", lambda e, tt=tt: e.dma_start(
                out=H["out"][128 * tt:128 * tt + 128, :],
                in_=x[tt % NCH][:, 1:1 + C]),
                reads=[("x", (tt % NCH, None))], writes=[("outhbm", tt)])
        return
    P.push()
    idf = P.sbr("idf5", [128, 128], F32)
    ostage = [P.sbr(f"ost{i}", [128, C], F32) for i in range(2)]
    em.op("SD", lambda e: e.dma_start(out=idf[:], in_=H["ident_f32"]),
          writes=[("idf5", None)])
    for tt in range(16):
        st = ostage[tt % 2]
        for c in range(NCH):
            bk = c % 3
            em.op("T", lambda e, c=c, tt=tt, bk=bk: e.transpose(
                P.bank(bk, 128), x[c][:, 1 + 128 * tt:1 + 128 * tt + 128], idf[:]),
                reads=[("x", (c, None)), ("idf5", None)], writes=[pk(bk)])
            em.op("V", lambda e, st=st, c=c, bk=bk, tt=tt: e.tensor_copy(
                st[:, 128 * c:128 * c + 128], P.bank(bk, 128)),
                reads=[pk(bk)], writes=[("ost", tt % 2)])
        em.op("SD", lambda e, st=st, tt=tt: e.dma_start(
            out=H["out"][128 * tt:128 * tt + 128, :], in_=st[:]),
            reads=[("ost", tt % 2)], writes=[("outhbm", tt)])
    P.pop()



def _attn_a_half(P, x, g, q, k, v_tm, half, projstop=None):
    em, H = P.em, P.hbm
    hb = 1024 * half
    P.push()
    xa = [P.sbr(f"xah{c}", [128, 1024], BF16) for c in range(NCH)]
    skipxa = DBG.get("aa_skipxa")
    P.push()
    rows = (P.sbr("lnraa", [1, 1024], F32), P.sbr("lnrab", [1, 1024], F32), P.sbr("lnras", [1, 1024], F32))
    g["a_sqt"] = [P.sbr(f"asqt{i}", [128, 512], BF16) for i in range(2)]
    xbt = [P.sbr(f"axbt{i}", [128, 512], BF16) for i in range(2)]
    tmp = [P.sbr(f"atmp{i}", [128, 512], F32) for i in range(2)]

    def src(c, j):
        i = (2 * c + j) % 2
        em.op("V", lambda e, c=c, j=j, i=i: e.tensor_copy(
            xbt[i][:], x[c][:, 1 + hb + 512 * j:1 + hb + 512 * j + 512]),
            reads=[("x", (c, None))], writes=[("axbt", i)])
        return [("axbt", i)], xbt[i][:]
    if not skipxa:
        ln_half(P, g, rows, half, src, "a_")
        for j in range(2):
            ln_bcast(P, g, rows, j, 0, 1)
            for c in range(NCH):
                i = (2 * c + j) % 2
                slp = slice(1 + hb + 512 * j, 1 + hb + 512 * j + 512)
                em.op("V", lambda e, c=c, i=i, slp=slp: e.tensor_tensor(
                    out=tmp[i][:], in0=x[c][:, slp], in1=P.bank(0), op=ALU.mult),
                    reads=[("x", (c, None)), pk(0)], writes=[("atmp", i)])
                em.op("V", lambda e, i=i: e.tensor_tensor(
                    out=tmp[i][:], in0=tmp[i][:], in1=P.bank(1), op=ALU.add),
                    reads=[("atmp", i), pk(1)], writes=[("atmp", i)])
                em.op("S", lambda e, c=c, j=j, i=i: e.activation(
                    out=xa[c][:, 512 * j:512 * j + 512], in_=tmp[i][:], func=AF.Tanh),
                    reads=[("atmp", i)], writes=[("xah", (c, j))])
    else:
        for c in range(NCH):
            em.op("V", lambda e, c=c: e.memset(xa[c][:], 0.125),
                  writes=[("xah", (c, None))])
    P.pop()

    if projstop == 0:
        P.pop()
        return
    P.push()
    xsq_hi = P.sbr("xsqh", [1, 1024], BF16)
    xsq_lo = P.sbr("xsql", [1, 1024], BF16)
    sq2 = [P.sbr(f"asq2{i}", [128, 512], BF16) for i in range(2)]
    wrow = P.sbr("wrow", [128, C], F32)
    wtan = P.sbr("wtan", [128, C], BF16)
    wsqs = P.sbr("wsqs", [128, 512], BF16)
    wsqh = P.sbr("wsqh", [128, 2], F32)
    wsq = P.sbr("wsqc", [128, 1], F32)
    waT = P.sbr("waT", [128, C], BF16)
    vft = P.sbr("vft", [128, 512], BF16)
    gcol = P.sbr("gcol", [128, 8], F32)
    s1c = P.sbr("s1c", [128, 8], F32)
    s2c = P.sbr("s2c", [128, 8], F32)
    skipxsq = DBG.get("aa_skipxsq")
    if skipxsq:
        em.op("V", lambda e: e.memset(xsq_hi[:], 0.5), writes=[("xsq", ("h", 0)), ("xsq", ("h", 1))])
        em.op("V", lambda e: e.memset(xsq_lo[:], 0.0), writes=[("xsq", ("l", 0)), ("xsq", ("l", 1))])
    for c in range(NCH if not skipxsq else 0):
        for j in range(2):
            i = (2 * c + j) % 2
            em.op("S", lambda e, c=c, j=j, i=i: e.activation(
                out=sq2[i][:], in_=xa[c][:, 512 * j:512 * j + 512], func=AF.Square),
                reads=[("xah", (c, j))], writes=[("asq2", i)])
            em.op("T", lambda e, j=j, i=i, c=c: e.matmul(
                P.bank(3 + j, p0=0, p=1), g["ones_col_b"][:], sq2[i][:],
                start=(c == 0), stop=(c == NCH - 1)),
                reads=[("asq2", i), ("ones_col_b", None)], writes=[pk(3 + j, "r0")])
    for j in range(2 if not skipxsq else 0):
        sl = slice(512 * j, 512 * j + 512)
        em.op("V", lambda e, j=j, sl=sl: e.tensor_copy(
            xsq_hi[0:1, sl], P.bank(3 + j, p0=0, p=1)),
            reads=[pk(3 + j, "r0")], writes=[("xsq", ("h", j))])
        em.op("V", lambda e, j=j, sl=sl: e.tensor_tensor(
            out=xsq_lo[0:1, sl], in0=P.bank(3 + j, p0=0, p=1), in1=xsq_hi[0:1, sl],
            op=ALU.subtract),
            reads=[pk(3 + j, "r0"), ("xsq", ("h", j))], writes=[("xsq", ("l", j))])

    for pi_, (nm, wnm, gnm, fn) in enumerate((("aq", "aWq", "agq", AF.Tanh),
                             ("ak", "aWk", "agk", AF.Tanh),
                             ("av", "aWv", "agv", AF.Identity))):
        if projstop is not None and pi_ >= projstop - 1:
            break
        dst = {"aq": q, "ak": k, "av": None}[nm]
        em.op("SD", lambda e, gnm=gnm: e.dma_start(
            out=gcol[:], in_=H[gnm].rearrange("(k p) -> p k", p=128),
            allow_slow_non_contiguous=True),
            writes=[("gcol", None)])
        em.op("V", lambda e: e.tensor_scalar(
            out=s1c[:], in0=gcol[:], scalar1=-1.0 / C, scalar2=None, op0=ALU.mult),
            reads=[("gcol", None)], writes=[("s1c", None)])
        em.op("V", lambda e: e.tensor_scalar(
            out=s2c[:], in0=gcol[:], scalar1=0.5, scalar2=None, op0=ALU.mult),
            reads=[("gcol", None)], writes=[("s2c", None)])
        psub = DBG.get("aa_projsub")
        for ot in range(DBG.get("aa_nots", 8)):
            em.op("SD", lambda e, wnm=wnm, ot=ot: e.dma_start(
                out=wrow[:], in_=H[wnm][128 * ot:128 * ot + 128, :]),
                writes=[("wrow", None)])
            em.op("S", lambda e: e.activation(
                out=wtan[:], in_=wrow[:], func=AF.Tanh),
                reads=[("wrow", None)], writes=[("wtan", None)])
            for hf in range(2):
                em.op("S", lambda e, hf=hf: e.activation(
                    out=wsqs[:], in_=wtan[:, 512 * hf:512 * hf + 512],
                    func=AF.Square, accum_out=wsqh[:, hf:hf + 1]),
                    reads=[("wtan", None)],
                    writes=[("wsqs", None), ("wsqh", hf)])
            em.op("V", lambda e: e.tensor_tensor(
                out=wsq[:], in0=wsqh[:, 0:1], in1=wsqh[:, 1:2], op=ALU.add),
                reads=[("wsqh", 0), ("wsqh", 1)], writes=[("wsqc", None)])
            if psub is not None and psub < 2:
                continue
            em.op("V", lambda e: e.tensor_scalar(
                out=wtan[:], in0=wtan[:], scalar1=-2.0, scalar2=None, op0=ALU.mult),
                reads=[("wtan", None)], writes=[("wtan", None)])
            if psub is not None and psub < 3:
                continue
            for grp in range(DBG.get("aa_ngrp", 2)):
                for c4 in range(DBG.get("aa_ntr", 4)):
                    c = 4 * grp + c4
                    pb = c4 * 256
                    em.op("T", lambda e, c=c, pb=pb: e.transpose(
                        P.psb[:, pb:pb + 128], wtan[:, 128 * c:128 * c + 128],
                        g["ident_bf"][:]),
                        reads=[("wtan", None), ("ident_bf", None), ("psbg", None)],
                        writes=[("psb", c4), ("psbL", None)])
                for c4 in range(DBG.get("aa_ntr", 4)):
                    c = 4 * grp + c4
                    pb = c4 * 256
                    em.op("V", lambda e, c=c, pb=pb: e.tensor_copy(
                        waT[:, 128 * c:128 * c + 128], P.psb[:, pb:pb + 128]),
                        reads=[("psb", c4), ("psbL", None)],
                        writes=[("waT", c)]
                        + ([("psbg", None)] if c4 == DBG.get("aa_ntr", 4) - 1 else []))
            if psub is not None and psub < 4:
                continue
            for j in range(2):
                bk = j
                sl = slice(512 * j, 512 * j + 512)
                slp = slice(1 + hb + 512 * j, 1 + hb + 512 * j + 512)

                def chain(e, sl=sl, bk=bk, xh_=xsq_hi, xl_=xsq_lo):
                    for c in range(NCH):
                        e.matmul(P.bank(bk), waT[:, 128 * c:128 * c + 128],
                                 xa[c][:, sl], start=(c == 0), stop=False)
                    e.matmul(P.bank(bk), g["ones_row_b"][0:1, :], xh_[0:1, sl],
                             start=False, stop=False)
                    return e.matmul(P.bank(bk), g["ones_row_b"][0:1, :],
                                    xl_[0:1, sl], start=False, stop=True)
                em.op("T", chain,
                      reads=[("waT", c) for c in range(NCH)]
                      + [("xah", (c, None)) for c in range(NCH)]
                      + [("xsq", None), ("ones_row_b", None)],
                      writes=[pk(bk)])
                if psub is not None and psub < 5:
                    continue
                em.op("V", lambda e, bk=bk: e.tensor_scalar(
                    out=P.bank(bk), in0=P.bank(bk), scalar1=wsq[:], scalar2=0.0,
                    op0=ALU.add, op1=ALU.max),
                    reads=[pk(bk), ("wsqc", None)], writes=[pk(bk)])
                if psub is not None and psub < 6:
                    continue
                if dst is not None:
                    em.op("S", lambda e, dst=dst, ot=ot, slp=slp, bk=bk, fn=fn:
                          e.activation(out=dst[ot][:, slp], in_=P.bank(bk),
                                       func=fn, bias=s2c[:, ot:ot + 1],
                                       scale=s1c[:, ot:ot + 1]),
                          reads=[pk(bk), ("s1c", None), ("s2c", None)],
                          writes=[(nm, (ot, half * 2 + j))])
                elif psub == 9:
                    em.op("V", lambda e: e.memset(vft[:], 0.5),
                          reads=[], writes=[("vft", None)])
                elif psub is None or psub >= 7:
                    em.op("V", lambda e, ot=ot, bk=bk: e.tensor_scalar(
                        out=vft[:], in0=P.bank(bk), scalar1=s1c[:, ot:ot + 1],
                        scalar2=s2c[:, ot:ot + 1], op0=ALU.mult, op1=ALU.add),
                        reads=[pk(bk), ("s1c", None), ("s2c", None)],
                        writes=[("vft", None)])
                    for i4 in range(4):
                        kb = 8 * half + 4 * j + i4
                        pb = (i4 % 4) * 256
                        em.op("T", lambda e, i4=i4, pb=pb: e.transpose(
                            P.psb[:, pb:pb + 128],
                            vft[:, 128 * i4:128 * i4 + 128], g["ident_bf"][:]),
                            reads=[("vft", None), ("ident_bf", None),
                                   ("psbg", None)],
                            writes=[("psb", i4 % 4), ("psbL", None)])
                        em.op("V", lambda e, kb=kb, ot=ot, pb=pb: e.tensor_copy(
                            v_tm[kb][:, 65 * 2 * ot:65 * 2 * ot + 64],
                            P.psb[:, pb:pb + 64]),
                            reads=[("psb", i4 % 4), ("psbL", None)],
                            writes=[("avt", (kb, 2 * ot))])
                        em.op("V", lambda e, kb=kb, ot=ot, pb=pb: e.tensor_copy(
                            v_tm[kb][:, 65 * (2 * ot + 1):65 * (2 * ot + 1) + 64],
                            P.psb[:, pb + 64:pb + 128]),
                            reads=[("psb", i4 % 4), ("psbL", None)],
                            writes=[("avt", (kb, 2 * ot + 1))]
                            + ([("psbg", None)] if i4 == 3 else []))
    P.pop()
    P.pop()



def attn_a(P, x, g):
    em, H = P.em, P.hbm
    P.push()
    q = [P.sbr(f"aq{c}", [128, TP], BF16) for c in range(NCH)]
    k = [P.sbr(f"ak{c}", [128, TP], BF16) for c in range(NCH)]
    v_tm = [P.sbr(f"avt{t}", [128, 16 * 65], BF16) for t in range(16)]

    for c in range(NCH):
        em.op("V", lambda e, c=c: e.memset(q[c][:, 0:1], 0.0), writes=[("aq", (c, "p"))])
        em.op("V", lambda e, c=c: e.memset(k[c][:, 0:1], 0.0), writes=[("ak", (c, "p"))])
    for kb in range(16):
        em.op("V", lambda e, kb=kb: e.memset(
            v_tm[kb][:].rearrange("p (h c) -> p h c", c=65)[:, :, 64:65], 1.0),
            writes=[("avt", (kb, "ones"))])

    aa = DBG.get("aa_stop")
    for half in range(2):
        if aa is None or aa >= 2 or (aa == 1 and half == 0):
            _attn_a_half(P, x, g, q, k, v_tm, half, DBG.get("aa_projstop"))

    if DBG.get("aa_stop") is not None and DBG["aa_stop"] < 3:
        P.pop()
        return
    out_a = [P.sbr(f"oa{c}", [128, T], BF16) for c in range(NCH)]
    P.push()
    ksqT = P.sbr("ksqT", [128, 256], F32)
    ksq_sq = [P.sbr("kss0", [128, 512], BF16)] * 2
    Ebuf = [P.sbr(f"aE{i}", [128, 1024], BF16) for i in range(2)]
    rrow = P.sbr("arr", [1, 512], F32)
    rb = P.sbr("arb", [64, 512], F32)
    for c in range(NCH):
        for tt in range(4):
            i = (4 * c + tt) % 2
            em.op("S", lambda e, c=c, tt=tt, i=i: e.activation(
                out=ksq_sq[i][:], in_=k[c][:, dsl(tt)], func=AF.Square),
                reads=[("ak", (c, None))], writes=[("kss", None)])
            for j4 in range(4):
                kb = 4 * tt + j4
                for hh in range(2):
                    h = 2 * c + hh
                    em.op("T", lambda e, i=i, j4=j4, hh=hh, h=h, kb=kb: e.matmul(
                        P.bank(6, w=1, off=16 * h + kb),
                        ksq_sq[i][64 * hh:64 * hh + 64, 128 * j4:128 * j4 + 128],
                        g["ones_col_b"][64 * hh:64 * hh + 64, :],
                        start=True, stop=True, tile_position=(64 * hh, 0)),
                        reads=[("kss", None), ("ones_col_b", None)],
                        writes=[pk(6, 16 * h + kb)])
    em.op("V", lambda e: e.tensor_scalar(
        out=ksqT[:], in0=P.bank(6, w=256), scalar1=-1.0 / 80.0, scalar2=None,
        op0=ALU.mult),
        reads=[pk(6, None)], writes=[("ksqT", None)])

    for h in range(16):
        ch, hh = h // 2, h % 2
        po = 64 * hh
        for qtp in range(2):
            for kb in range(16):
                sp = 2 + 2 * (kb % 2)
                for i in range(2):
                    qt = 2 * qtp + i
                    em.op("T", lambda e, ch=ch, po=po, kb=kb, sp=sp, i=i, qt=qt:
                          e.matmul(P.bank(sp + i),
                                   k[ch][po:po + 64, 1 + 128 * kb:1 + 128 * kb + 128],
                                   q[ch][po:po + 64, dsl(qt)],
                                   start=True, stop=True, tile_position=(po, 0)),
                          reads=[("ak", (ch, None)), ("aq", (ch, None))],
                          writes=[pk(sp + i)])
                eb = Ebuf[kb % 2]
                em.op("S", lambda e, sp=sp, h=h, kb=kb, eb=eb: e.activation(
                    out=eb[:], in_=P.ps[:, 512 * sp:512 * sp + 1024], func=AF.Exp,
                    bias=ksqT[:, 16 * h + kb:16 * h + kb + 1], scale=0.025),
                    reads=[pk(sp), pk(sp + 1), ("ksqT", None)],
                    writes=[("aE", kb % 2)])
                for i in range(2):
                    em.op("T", lambda e, kb=kb, h=h, i=i, eb=eb: e.matmul(
                        P.bank(0 + i, p=65), v_tm[kb][:, 65 * h:65 * h + 65],
                        eb[:, 512 * i:512 * i + 512],
                        start=(kb == 0), stop=(kb == 15)),
                        reads=[("aE", kb % 2), ("avt", (kb, h)),
                               ("avt", (kb, "ones"))],
                        writes=[pk(0 + i)])
            for i in range(2):
                qt = 2 * qtp + i
                em.op("V", lambda e, i=i: e.reciprocal(
                    rrow[0:1, :], P.bank(0 + i, p0=64, p=1)),
                    reads=[pk(0 + i)], writes=[("arr", None)])
                em.op("T", lambda e, i=i: e.matmul(
                    P.bank(6, p=64), g["ones_row_f"][0:1, 0:64],
                    rrow[0:1, :], start=True, stop=True),
                    reads=[("arr", None), ("ones_row_f", None)], writes=[pk(6)])
                em.op("V", lambda e: e.tensor_copy(rb[:], P.bank(6, p=64)),
                      reads=[pk(6)], writes=[("arb", None)])
                em.op("V", lambda e, ch=ch, hh=hh, qt=qt, i=i: e.tensor_tensor(
                    out=out_a[ch][64 * hh:64 * hh + 64, 512 * qt:512 * qt + 512],
                    in0=P.bank(0 + i, p=64), in1=rb[:], op=ALU.mult),
                    reads=[pk(0 + i), ("arb", None)], writes=[("oa", (ch, (hh, qt)))])
    P.pop()

    if DBG.get("aa_stop") is not None and DBG["aa_stop"] < 4:
        P.pop()
        return
    P.push()
    worow = P.sbr("worow", [128, C], F32)
    wob = P.sbr("wob", [128, C], BF16)
    woT = P.sbr("woT", [128, C], BF16)
    for ot in range(8):
        em.op("SD", lambda e, ot=ot: e.dma_start(
            out=worow[:], in_=H["aWo"][128 * ot:128 * ot + 128, :]),
            writes=[("worow", None)])
        em.op("V", lambda e: e.tensor_copy(wob[:], worow[:]),
              reads=[("worow", None)], writes=[("wob", None)])
        for grp in range(2):
            for c4 in range(4):
                c = 4 * grp + c4
                pb = c4 * 256
                em.op("T", lambda e, c=c, pb=pb: e.transpose(
                    P.psb[:, pb:pb + 128], wob[:, 128 * c:128 * c + 128],
                    g["ident_bf"][:]),
                    reads=[("wob", None), ("ident_bf", None), ("psbg", None)],
                    writes=[("psb", c4), ("psbL", None)])
            for c4 in range(4):
                c = 4 * grp + c4
                pb = c4 * 256
                em.op("V", lambda e, c=c, pb=pb: e.tensor_copy(
                    woT[:, 128 * c:128 * c + 128], P.psb[:, pb:pb + 128]),
                    reads=[("psb", c4), ("psbL", None)],
                    writes=[("woT", c)] + ([("psbg", None)] if c4 == 3 else []))
        for tt in range(4):
            bk = tt % 2

            def chain(e, tt=tt, bk=bk):
                last = None
                for c in range(NCH):
                    last = e.matmul(P.bank(bk), woT[:, 128 * c:128 * c + 128],
                                    out_a[c][:, 512 * tt:512 * tt + 512],
                                    start=(c == 0), stop=(c == NCH - 1))
                return last
            em.op("T", chain,
                  reads=[("woT", c) for c in range(NCH)]
                  + [("oa", (c, None)) for c in range(NCH)],
                  writes=[pk(bk)])
            em.op("V", lambda e, ot=ot, tt=tt, bk=bk: e.scalar_tensor_tensor(
                out=x[ot][:, dsl(tt)], in0=P.bank(bk), scalar=0.0, in1=x[ot][:, dsl(tt)],
                op0=ALU.add, op1=ALU.add),
                reads=[pk(bk), ("x", (ot, None))], writes=[("x", (ot, None))])
    P.pop()
    P.pop()



def xp_view(xp, c, start, w=512):
    off = 1 + start if c < 4 else start
    return xp[c][:, off:off + w]


def _ffn_ln_init(P, x, g, xp, half):
    em = P.em
    hb = 1024 * half
    P.push()
    rows = (P.sbr("flnra", [1, 1024], F32), P.sbr("flnrb", [1, 1024], F32), P.sbr("flnrs", [1, 1024], F32))
    g["f_sqt"] = [P.sbr(f"fsqt{i}", [128, 512], BF16) for i in range(2)]
    xbt = [P.sbr(f"fxbt{i}", [128, 512], BF16) for i in range(2)]
    tmp = [P.sbr(f"ftmp{i}", [128, 512], F32) for i in range(2)]

    def src(c, j):
        i = (2 * c + j) % 2
        em.op("V", lambda e, c=c, j=j, i=i: e.tensor_copy(
            xbt[i][:], x[c][:, 1 + hb + 512 * j:1 + hb + 512 * j + 512]),
            reads=[("x", (c, None))], writes=[("fxbt", i)])
        return [("fxbt", i)], xbt[i][:]
    ln_half(P, g, rows, half, src, "f_")
    for j in range(2):
        ln_bcast(P, g, rows, j, 0, 1)
        for c in range(NCH):
            i = (2 * c + j) % 2
            slp = slice(1 + hb + 512 * j, 1 + hb + 512 * j + 512)
            em.op("V", lambda e, c=c, i=i, slp=slp: e.tensor_tensor(
                out=tmp[i][:], in0=x[c][:, slp], in1=P.bank(0), op=ALU.mult),
                reads=[("x", (c, None)), pk(0)], writes=[("ftmp", i)])
            em.op("V", lambda e, c=c, i=i, slp=slp: e.tensor_tensor(
                out=xp[c][:, slp], in0=tmp[i][:], in1=P.bank(1), op=ALU.add),
                reads=[("ftmp", i), pk(1)], writes=[("xp", (c, None))])
    P.pop()


def _ffn_pair(P, g, xp, tb, v_tm, o_acc, call, l, p):
    em, H = P.em, P.hbm
    P.push()
    wq = P.sbr("fwq", [128, 1024], BF16)
    wk = P.sbr("fwk", [128, 1024], BF16)
    wv = P.sbr("fwv", [128, 1024], BF16)
    qp = P.sbr("fqp", [128, T], BF16)
    kp = P.sbr("fkp", [128, T], BF16)
    sqs = P.sbr("fsqs", [128, 512], BF16)
    qsr = P.sbr("fqsr", [2, 512], F32)
    RSQb = P.sbr("frsq", [128, 512], F32)
    ut = P.sbr("fut", [128, 512], BF16)
    t1 = P.sbr("ft1", [128, 512], BF16)
    t2 = P.sbr("ft2", [128, 512], BF16)
    vft = P.sbr("fvft", [128, 512], BF16)
    Ebuf = [P.sbr(f"fE{i}", [128, 512], BF16) for i in range(2)]
    sadd = P.sbr("fsadd", [128, 512], F32)
    rrow = P.sbr("frr", [1, 512], F32)
    rb = P.sbr("frb", [64, 512], F32)

    for nm, wdst, col0 in (("q", wq, 128 * p), ("k", wk, 512 + 128 * p),
                           ("v", wv, 1024 + 128 * p)):
        em.op("GD", lambda e, wdst=wdst, col0=col0: e.dma_start(
            out=wdst[:], in_=H["fwqkv"][l, :, col0:col0 + 128].rearrange(
                "(cc pp) c -> pp cc c", pp=128)),
            writes=[(f"fw{nm}", None)])

    for nm, wsrc, pt, scol in (("q", wq, qp, 0), ("k", wk, kp, 64)):
        for tt in range(4):
            def chain(e, wsrc=wsrc, tt=tt):
                last = None
                for c in range(NCH):
                    last = e.matmul(P.bank(tt), wsrc[:, 128 * c:128 * c + 128],
                                    xp_view(xp, c, 512 * tt),
                                    start=(c == 0), stop=(c == NCH - 1))
                return last
            em.op("T", chain,
                  reads=[(f"fw{nm}", None)] + [("xp", (c, None)) for c in range(NCH)],
                  writes=[pk(tt)])
        for tt in range(4):
            sl = slice(512 * tt, 512 * tt + 512)
            em.op("S", lambda e, tt=tt: e.activation(
                out=sqs[:], in_=P.bank(tt), func=AF.Square),
                reads=[pk(tt)], writes=[("fsqs", None)])
            em.op("T", lambda e: e.matmul(
                P.bank(4, p=2), g["e2b"][:], sqs[:], start=True, stop=True),
                reads=[("fsqs", None), ("e2b", None)], writes=[pk(4)])
            em.op("S", lambda e: e.activation(
                out=qsr[:], in_=P.bank(4, p=2), func=AF.Sqrt),
                reads=[pk(4)], writes=[("fqsr", None)])
            em.op("V", lambda e: e.tensor_scalar(
                out=qsr[:], in0=qsr[:], scalar1=1e-12, scalar2=None, op0=ALU.max),
                reads=[("fqsr", None)], writes=[("fqsr", None)])
            em.op("V", lambda e: e.reciprocal(qsr[:], qsr[0:2, :]),
                reads=[("fqsr", None)], writes=[("fqsr", None)])
            em.op("T", lambda e: e.matmul(
                P.bank(5), g["e2f"][:], qsr[:], start=True, stop=True),
                reads=[("fqsr", None), ("e2f", None)], writes=[pk(5)])
            sccol = 0 if scol == 0 else 1
            em.op("V", lambda e, sccol=sccol: e.tensor_scalar(
                out=RSQb[:], in0=P.bank(5), scalar1=tb["scv"][:, sccol:sccol + 1],
                scalar2=None, op0=ALU.mult),
                reads=[pk(5), ("fscv", None)], writes=[("frsq", None)])
            em.op("V", lambda e, tt=tt: e.tensor_tensor(
                out=ut[:], in0=P.bank(tt), in1=RSQb[:], op=ALU.mult),
                reads=[pk(tt), ("frsq", None)], writes=[("fut", None)])
            em.op("T", lambda e: e.matmul(
                P.bank(6), g["rotp"][:], ut[:], start=True, stop=True),
                reads=[("fut", None), ("rotp", None)], writes=[pk(6)])
            cst = tb["csq"] if nm == "q" else tb["csk"]
            sst = tb["ssq"] if nm == "q" else tb["ssk"]
            csk_ = "fcsq" if nm == "q" else "fcsk"
            ssk_ = "fssq" if nm == "q" else "fssk"
            em.op("V", lambda e, sl=sl, cst=cst, csk_=csk_: e.tensor_tensor(
                out=t1[:], in0=ut[:], in1=cst[:, sl], op=ALU.mult),
                reads=[("fut", None), (csk_, None)], writes=[("ft1", None)])
            em.op("V", lambda e, sl=sl, sst=sst, ssk_=ssk_: e.tensor_tensor(
                out=t2[:], in0=P.bank(6), in1=sst[:, sl], op=ALU.mult),
                reads=[pk(6), (ssk_, None)], writes=[("ft2", None)])
            em.op("V", lambda e, pt=pt, sl=sl: e.tensor_tensor(
                out=pt[:, sl], in0=t1[:], in1=t2[:], op=ALU.add),
                reads=[("ft1", None), ("ft2", None)],
                writes=[("fqp" if True else "", tt) if pt is qp else ("fkp", tt)])

    for tt in range(4):
        def chainv(e, tt=tt):
            last = None
            for c in range(NCH):
                last = e.matmul(P.bank(tt % 2), wv[:, 128 * c:128 * c + 128],
                                xp_view(xp, c, 512 * tt),
                                start=(c == 0), stop=(c == NCH - 1))
            return last
        em.op("T", chainv,
              reads=[("fwv", None)] + [("xp", (c, None)) for c in range(NCH)],
              writes=[pk(tt % 2)])
        em.op("V", lambda e, tt=tt: e.tensor_copy(vft[:], P.bank(tt % 2)),
              reads=[pk(tt % 2)], writes=[("fvft", None)])
        for i4 in range(4):
            pb = i4 * 256
            em.op("T", lambda e, i4=i4, pb=pb: e.transpose(
                P.psb[:, pb:pb + 128], vft[:, 128 * i4:128 * i4 + 128],
                g["ident_bf"][:]),
                reads=[("fvft", None), ("ident_bf", None), ("psbg", None)],
                writes=[("psb", i4), ("psbL", None)])
        for i4 in range(4):
            kb = 4 * tt + i4
            pb = i4 * 256
            for hf in range(2):
                em.op("V", lambda e, kb=kb, pb=pb, hf=hf: e.tensor_copy(
                    v_tm[kb][:, 65 * hf:65 * hf + 64],
                    P.psb[:, pb + 64 * hf:pb + 64 * hf + 64]),
                    reads=[("psb", i4), ("psbL", None)], writes=[("fvt", (kb, hf))]
                    + ([("psbg", None)] if (i4 == 3 and hf == 1) else []))

    for hf in range(2):
        po = 64 * hf
        for qt in range(4):
            kbs = list(range(max(0, 4 * qt - 4), min(16, 4 * qt + 4)))
            for ki, kb in enumerate(kbs):
                sb = 2 + (ki % 2)
                em.op("T", lambda e, po=po, kb=kb, qt=qt, sb=sb: e.matmul(
                    P.bank(sb), kp[po:po + 64, 128 * kb:128 * kb + 128],
                    qp[po:po + 64, 512 * qt:512 * qt + 512],
                    start=True, stop=True, tile_position=(po, 0)),
                    reads=[("fkp", None), ("fqp", None)], writes=[pk(sb)])
                delta = 512 * qt - 128 * kb
                eb = Ebuf[ki % 2]
                ms = slice(delta + 384, delta + 384 + 512)
                if delta <= 0:
                    em.op("V", lambda e, sb=sb, ms=ms: e.tensor_tensor(
                        out=sadd[:], in0=P.bank(sb), in1=tb["wadd"][:, ms],
                        op=ALU.add),
                        reads=[pk(sb), ("fwadd", None)], writes=[("fsadd", None)])
                    em.op("S", lambda e, eb=eb: e.activation(
                        out=eb[:], in_=sadd[:], func=AF.Exp, scale=8.0),
                        reads=[("fsadd", None)], writes=[("fE", ki % 2)])
                else:
                    em.op("S", lambda e, sb=sb, eb=eb: e.activation(
                        out=eb[:], in_=P.bank(sb), func=AF.Exp, scale=8.0),
                        reads=[pk(sb)], writes=[("fE", ki % 2)])
                    em.op("V", lambda e, eb=eb, ms=ms: e.tensor_tensor(
                        out=eb[:], in0=eb[:], in1=tb["wmul"][:, ms], op=ALU.mult),
                        reads=[("fE", ki % 2), ("fwmul", None)],
                        writes=[("fE", ki % 2)])
                em.op("T", lambda e, kb=kb, hf=hf, eb=eb, st=(ki == 0),
                      sp=(ki == len(kbs) - 1): e.matmul(
                    P.bank(0, p=65), v_tm[kb][:, 65 * hf:65 * hf + 65],
                    eb[:], start=st, stop=sp),
                    reads=[("fE", ki % 2), ("fvt", (kb, hf)), ("fvt", (kb, "o"))],
                    writes=[pk(0)])
            em.op("V", lambda e: e.reciprocal(rrow[0:1, :], P.bank(0, p0=64, p=1)),
                  reads=[pk(0)], writes=[("frr", None)])
            em.op("T", lambda e: e.matmul(
                P.bank(5, p=64), g["ones_row_f"][0:1, 0:64], rrow[0:1, :],
                start=True, stop=True),
                reads=[("frr", None), ("ones_row_f", None)], writes=[pk(5)])
            em.op("V", lambda e: e.tensor_copy(rb[:], P.bank(5, p=64)),
                  reads=[pk(5)], writes=[("frb", None)])
            em.op("V", lambda e, p=p, po=po, qt=qt: e.tensor_tensor(
                out=o_acc[p][po:po + 64, 512 * qt:512 * qt + 512],
                in0=P.bank(0, p=64), in1=rb[:], op=ALU.mult),
                reads=[pk(0), ("frb", None)], writes=[("oac", (p, (po, qt)))])
    P.pop()


def _ffn_wo(P, g, xp, o_acc, call, l):
    em, H = P.em, P.hbm
    P.push()
    wot = [P.sbr(f"fwot{i}", [128, 512], BF16) for i in range(2)]
    for ot in range(8):
        r = ot % 2
        em.op("GD", lambda e, ot=ot, r=r: e.dma_start(
            out=wot[r][:], in_=H["fwo"][l, :, 128 * ot:128 * ot + 128].rearrange(
                "(cc pp) c -> pp cc c", pp=128)),
            writes=[("fwot", r)])
        for tt in range(4):
            bk = tt % 2
            def chain(e, r=r, tt=tt, bk=bk):
                last = None
                for c in range(4):
                    last = e.matmul(P.bank(bk), wot[r][:, 128 * c:128 * c + 128],
                                    o_acc[c][:, 512 * tt:512 * tt + 512],
                                    start=(c == 0), stop=(c == 3))
                return last
            em.op("T", chain,
                  reads=[("fwot", r)] + [("oac", (c, None)) for c in range(4)],
                  writes=[pk(bk)])
            em.op("V", lambda e, ot=ot, tt=tt, bk=bk: e.scalar_tensor_tensor(
                out=xp[ot][:, dsl(tt)], in0=P.bank(bk), scalar=0.0,
                in1=xp[ot][:, dsl(tt)], op0=ALU.add, op1=ALU.add),
                reads=[pk(bk), ("xp", (ot, None))], writes=[("xp", (ot, None))])
    P.pop()


def _ffn_layer_attn(P, x, g, xp, call, l):
    em, H = P.em, P.hbm
    P.push()
    tb = {}
    tb["csq"] = P.sbr("fcsq", [128, T], BF16)
    tb["csk"] = P.sbr("fcsk", [128, T], BF16)
    tb["ssq"] = P.sbr("fssq", [128, T], BF16)
    tb["ssk"] = P.sbr("fssk", [128, T], BF16)
    tb["wadd"] = P.sbr("fwaddt", [128, 1408], BF16)
    tb["wmul"] = P.sbr("fwmult", [128, 1408], BF16)
    tb["scv"] = P.sbr("fscv", [128, 2], F32)
    o_acc = [P.sbr(f"foac{c}", [128, T], BF16) for c in range(4)]
    v_tm = [P.sbr(f"fvt{t}", [128, 130], BF16) for t in range(16)]
    for nm, key in (("rot_csq", "fcsq"), ("rot_csk", "fcsk"),
                    ("rot_ssq", "fssq"), ("rot_ssk", "fssk"),
                    ("win_add", "fwadd"), ("win_mul", "fwmul")):
        dstt = tb[key[1:]]
        em.op("SD", lambda e, nm=nm, dstt=dstt: e.dma_start(out=dstt[:], in_=H[nm]),
              writes=[(key, None)])
    for hf in range(2):
        em.op("SD", lambda e, hf=hf: e.dma_start(
            out=tb["scv"][64 * hf:64 * hf + 64, 0:1],
            in_=H["fqs"][l, :].rearrange("(k p) -> p k", p=64),
            allow_slow_non_contiguous=True),
            writes=[("fscv", None)])
        em.op("SD", lambda e, hf=hf: e.dma_start(
            out=tb["scv"][64 * hf:64 * hf + 64, 1:2],
            in_=H["fks"][l, :].rearrange("(k p) -> p k", p=64),
            allow_slow_non_contiguous=True),
            writes=[("fscv", None)])
    for kb in range(16):
        for hf in range(2):
            em.op("V", lambda e, kb=kb, hf=hf: e.memset(
                v_tm[kb][:, 65 * hf + 64:65 * hf + 65], 1.0),
                writes=[("fvt", (kb, "o"))])
    for p in range(4):
        _ffn_pair(P, g, xp, tb, v_tm, o_acc, call, l, p)
    _ffn_wo(P, g, xp, o_acc, call, l)
    P.pop()


def _ffn_layer_mlp(P, g, xp, stash, call, l, th):
    em, H = P.em, P.hbm
    hb = 1024 * th
    P.push()
    rows = (P.sbr("fmra", [1, 1024], F32), P.sbr("fmrb", [1, 1024], F32), P.sbr("fmrs", [1, 1024], F32))
    g["m_sqt"] = [P.sbr(f"msqt{i}", [128, 512], BF16) for i in range(2)]
    tmp = [P.sbr(f"mtmp{i}", [128, 512], F32) for i in range(2)]
    gcol = P.sbr("mgc", [128, 8], F32)
    lbc = P.sbr("mlbc", [128, 8], BF16)
    b1a = P.sbr("mb1a", [128, 22], F32)
    b1g = P.sbr("mb1g", [128, 22], F32)
    b2c = P.sbr("mb2c", [128, 8], F32)
    b1p = P.sbr("mb1p", [128, 1], F32)
    xh = [P.sbr(f"mxh{c}", [128, 1024], BF16) for c in range(NCH)]
    DBG["names"][f"mxh{call}{l}{th}"] = [t.name for t in xh]
    h2 = [P.sbr(f"mh2{c}", [128, 1024], BF16) for c in range(22)]
    abuf = P.sbr("mab", [128, 1024], BF16)
    gbuf = P.sbr("mgb", [128, 1024], BF16)
    w1t = [P.sbr(f"mw1{i}", [128, 1024], BF16) for i in range(2)]
    w2t = P.sbr("mw2", [128, 2816], BF16)

    em.op("SD", lambda e: e.dma_start(
        out=gcol[:], in_=H["flng"][l, :].rearrange("(k p) -> p k", p=128),
        allow_slow_non_contiguous=True), writes=[("mgc", None)])
    em.op("GD", lambda e: e.dma_start(
        out=lbc[:], in_=H["flnb"][l, :].rearrange("(k p) -> p k", p=128),
        allow_slow_non_contiguous=True), writes=[("mlbc", None)])
    em.op("SD", lambda e: e.dma_start(
        out=b1a[:, 0:21], in_=H["fb1"][l, 0:2688].rearrange("(k p) -> p k", p=128),
        allow_slow_non_contiguous=True), writes=[("mb1a", None)])
    em.op("SD", lambda e: e.dma_start(
        out=b1a[0:42, 21:22], in_=H["fb1"][l, 2688:2730].rearrange(
            "(k p) -> p k", p=42), allow_slow_non_contiguous=True),
        writes=[("mb1a", None)])
    em.op("SD", lambda e: e.dma_start(
        out=b1g[:, 0:21], in_=H["fb1"][l, 2730:5418].rearrange(
            "(k p) -> p k", p=128), allow_slow_non_contiguous=True),
        writes=[("mb1g", None)])
    em.op("SD", lambda e: e.dma_start(
        out=b1g[0:42, 21:22], in_=H["fb1"][l, 5418:5460].rearrange(
            "(k p) -> p k", p=42), allow_slow_non_contiguous=True),
        writes=[("mb1g", None)])
    em.op("SD", lambda e: e.dma_start(
        out=b2c[:], in_=H["fb2"][l, :].rearrange("(k p) -> p k", p=128),
        allow_slow_non_contiguous=True), writes=[("mb2c", None)])

    spl = None
    if th == 1:
        spl = [P.sbr(f"mspl{c4}", [128, 512], BF16) for c4 in range(4)]
        for c4 in range(4):
            em.op("V", lambda e, c4=c4: e.tensor_copy(
                spl[c4][:], xp[4 + c4][:, 1024:1536]),
                reads=[("xp", (4 + c4, None))], writes=[("mspl", c4)])
            em.op("V", lambda e, c4=c4: e.tensor_copy(
                spl[c4][:, 0:1], stash[:, c4:c4 + 1]),
                reads=[("fstash", c4)], writes=[("mspl", c4)])

    def src(c, j):
        if th == 1 and c >= 4 and j == 0:
            return [("mspl", c - 4)], spl[c - 4][:]
        return [("xp", (c, None))], xp_view(xp, c, hb + 512 * j)
    ln_half(P, g, rows, th, src, "m_")
    for j in range(2):
        ln_bcast(P, g, rows, j, 0, 1)
        for c in range(NCH):
            i = (2 * c + j) % 2
            rk2, inap = ((["PLACE"], None))
            if th == 1 and c >= 4 and j == 0:
                rk2, inap = [("mspl", c - 4)], spl[c - 4][:]
            else:
                rk2, inap = [("xp", (c, None))], xp_view(xp, c, hb + 512 * j)
            em.op("V", lambda e, i=i, inap=inap: e.tensor_tensor(
                out=tmp[i][:], in0=inap, in1=P.bank(0), op=ALU.mult),
                reads=rk2 + [pk(0)], writes=[("mtmp", i)])
            em.op("V", lambda e, c=c, j=j, i=i: e.tensor_tensor(
                out=tmp[i][:], in0=tmp[i][:], in1=P.bank(1), op=ALU.add),
                reads=[("mtmp", i), pk(1)], writes=[("mtmp", i)])
            em.op("V", lambda e, c=c, j=j, i=i: e.tensor_scalar(
                out=xh[c][:, 512 * j:512 * j + 512], in0=tmp[i][:],
                scalar1=gcol[:, c:c + 1], scalar2=None, op0=ALU.mult),
                reads=[("mtmp", i), ("mgc", None)], writes=[("mxh", c)])
    for c22 in range(22):
        rn = 128 if c22 < 21 else 42
        for side in ("a", "g"):
            r = 0 if side == "a" else 1
            col0 = 128 * c22 + (0 if side == "a" else 2730)
            em.op("GD", lambda e, col0=col0, rn=rn, r=r: e.dma_start(
                out=w1t[r][:, 0:8 * rn],
                in_=H["fw1"][l, :, col0:col0 + rn].rearrange(
                    "(cc pp) c -> pp cc c", pp=128)),
                writes=[("mw1", r)])
            def blb(e, rn=rn, r=r):
                last = None
                for c in range(NCH):
                    last = e.matmul(P.bank(4, w=1, p=rn),
                                    w1t[r][:, rn * c:rn * c + rn],
                                    lbc[:, c:c + 1], start=(c == 0),
                                    stop=(c == NCH - 1))
                return last
            em.op("T", blb, reads=[("mw1", r), ("mlbc", None)], writes=[pk(4)])
            bsel = b1a if side == "a" else b1g
            em.op("V", lambda e, c22=c22, rn=rn, bsel=bsel: e.tensor_scalar(
                out=b1p[0:rn, :], in0=P.bank(4, w=1, p=rn),
                scalar1=bsel[0:rn, c22:c22 + 1] if False else None,
                scalar2=None, op0=ALU.add) if False else
                e.tensor_tensor(out=b1p[0:rn, :], in0=P.bank(4, w=1, p=rn),
                                in1=bsel[0:rn, c22:c22 + 1], op=ALU.add),
                reads=[pk(4), ("mb1a" if bsel is b1a else "mb1g", None)],
                writes=[("mb1p", None)])
            for j in range(2):
                bk = j
                def chain(e, rn=rn, r=r, j=j, bk=bk):
                    last = None
                    for c in range(NCH):
                        last = e.matmul(P.bank(bk, p=rn),
                                        w1t[r][:, rn * c:rn * c + rn],
                                        xh[c][:, 512 * j:512 * j + 512],
                                        start=(c == 0), stop=(c == NCH - 1))
                    return last
                em.op("T", chain,
                      reads=[("mw1", r), ("mb1p", None)]
                      + [("mxh", c) for c in range(NCH)],
                      writes=[pk(bk)])
                dstb = abuf if side == "a" else gbuf
                fn = AF.Identity if side == "a" else AF.Gelu
                em.op("S", lambda e, rn=rn, j=j, bk=bk, dstb=dstb, fn=fn:
                      e.activation(out=dstb[0:rn, 512 * j:512 * j + 512],
                                   in_=P.bank(bk, p=rn), func=fn,
                                   bias=b1p[0:rn, :], scale=1.0),
                      reads=[pk(bk), ("mb1p", None)],
                      writes=[("mab" if side == "a" else "mgb", j)])
        em.op("V", lambda e, c22=c22, rn=rn: e.tensor_tensor(
            out=h2[c22][0:rn, :], in0=abuf[0:rn, :], in1=gbuf[0:rn, :],
            op=ALU.mult),
            reads=[("mab", None), ("mgb", None)], writes=[("mh2", c22)])
    for ot in range(8):
        em.op("GD", lambda e, ot=ot: e.dma_start(
            out=w2t[:, 0:2688], in_=H["fw2"][l, 0:2688, 128 * ot:128 * ot + 128]
            .rearrange("(cc pp) c -> pp cc c", pp=128)),
            writes=[("mw2", 0)])
        em.op("GD", lambda e, ot=ot: e.dma_start(
            out=w2t[0:42, 2688:2816], in_=H["fw2"][l, 2688:2730,
                                                   128 * ot:128 * ot + 128]),
            writes=[("mw2", 1)])
        for j in range(2):
            bk = j
            def chain(e, j=j, bk=bk):
                last = None
                for c in range(22):
                    rn = 128 if c < 21 else 42
                    last = e.matmul(P.bank(bk), w2t[0:rn, 128 * c:128 * c + 128],
                                    h2[c][0:rn, 512 * j:512 * j + 512],
                                    start=(c == 0), stop=(c == 21))
                return last
            em.op("T", chain,
                  reads=[("mw2", None)] + [("mh2", c) for c in range(22)],
                  writes=[pk(bk)])
            em.op("V", lambda e, ot=ot, j=j, bk=bk: e.scalar_tensor_tensor(
                out=xp[ot][:, 1 + hb + 512 * j:1 + hb + 512 * j + 512],
                in0=P.bank(bk), scalar=b2c[:, ot:ot + 1],
                in1=xp[ot][:, 1 + hb + 512 * j:1 + hb + 512 * j + 512],
                op0=ALU.add, op1=ALU.add),
                reads=[pk(bk), ("mb2c", None), ("xp", (ot, None))],
                writes=[("xp", (ot, None))])
    P.pop()


def ffn_call(P, x, g, call):
    em = P.em
    P.push()
    xp = [P.sbr(f"xp{c}", [128, TP], BF16) for c in range(NCH)]
    DBG["names"][f"xp{call}"] = [t.name for t in xp]
    for c in range(NCH):
        em.op("V", lambda e, c=c: e.memset(xp[c][:, 0:1], 0.0),
              writes=[("xp", (c, "p"))])
    for half in range(2):
        _ffn_ln_init(P, x, g, xp, half)
    stash = P.sbr("fstash", [128, 4], BF16)
    step = 1
    stop = DBG.get("ffn_stop")
    for l in range(2):
        if stop is None or step < stop:
            _ffn_layer_attn(P, x, g, xp, call, l)
        step += 1
        for c4 in range(4):
            em.op("V", lambda e, c4=c4: e.tensor_copy(
                stash[:, c4:c4 + 1], xp[4 + c4][:, 1024:1025]),
                reads=[("xp", (4 + c4, None))], writes=[("fstash", c4)])
        for th in range(2):
            if stop is None or step < stop:
                _ffn_layer_mlp(P, g, xp, stash, call, l, th)
            step += 1
    for c in range(NCH):
        for tt in range(4):
            em.op("V", lambda e, c=c, tt=tt: e.tensor_tensor(
                out=x[c][:, dsl(tt)], in0=x[c][:, dsl(tt)], in1=xp[c][:, dsl(tt)],
                op=ALU.add),
                reads=[("x", (c, None)), ("xp", (c, None))],
                writes=[("x", (c, None))])
    if not DBG["keep"]:
        P.pop()



def _attn_b_half(P, x, g, q, k, v_tm, half):
    em, H = P.em, P.hbm
    hb = 1024 * half
    P.push()
    xh = [P.sbr(f"bxh{c}", [128, 1024], BF16) for c in range(NCH)]
    P.push()
    rows = (P.sbr("blnra", [1, 1024], F32), P.sbr("blnrb", [1, 1024], F32), P.sbr("blnrs", [1, 1024], F32))
    g["b_sqt"] = [P.sbr(f"bsqt{i}", [128, 512], BF16) for i in range(2)]
    xbt = [P.sbr(f"bxbt{i}", [128, 512], BF16) for i in range(2)]
    tmp = [P.sbr(f"btmp{i}", [128, 512], F32) for i in range(2)]

    def src(c, j):
        i = (2 * c + j) % 2
        em.op("V", lambda e, c=c, j=j, i=i: e.tensor_copy(
            xbt[i][:], x[c][:, 1 + hb + 512 * j:1 + hb + 512 * j + 512]),
            reads=[("x", (c, None))], writes=[("bxbt", i)])
        return [("bxbt", i)], xbt[i][:]
    ln_half(P, g, rows, half, src, "b_")
    for j in range(2):
        ln_bcast(P, g, rows, j, 0, 1)
        for c in range(NCH):
            i = (2 * c + j) % 2
            slp = slice(1 + hb + 512 * j, 1 + hb + 512 * j + 512)
            em.op("V", lambda e, c=c, i=i, slp=slp: e.tensor_tensor(
                out=tmp[i][:], in0=x[c][:, slp], in1=P.bank(0), op=ALU.mult),
                reads=[("x", (c, None)), pk(0)], writes=[("btmp", i)])
            em.op("V", lambda e, c=c, i=i, j=j: e.tensor_tensor(
                out=xh[c][:, 512 * j:512 * j + 512], in0=tmp[i][:], in1=P.bank(1),
                op=ALU.add),
                reads=[("btmp", i), pk(1)], writes=[("bxh", c)])
    P.pop()

    wcol = [P.sbr(f"bwc{i}", [128, 1024], BF16) for i in range(2)]
    vft = P.sbr("bvft", [128, 512], BF16)
    for nm, wnm, dst in (("q", "bWq", q), ("k", "bWk", k), ("v", "bWv", None)):
        for ot in range(8):
            r = ot % 2
            em.op("GD", lambda e, wnm=wnm, ot=ot, r=r: e.dma_start(
                out=wcol[r][:], in_=H[wnm][:, 128 * ot:128 * ot + 128].rearrange(
                    "(cc pp) c -> pp cc c", pp=128)),
                writes=[("bwc", r)])
            for j in range(2):
                bk = j
                def chain(e, r=r, j=j, bk=bk):
                    last = None
                    for c in range(NCH):
                        last = e.matmul(P.bank(bk), wcol[r][:, 128 * c:128 * c + 128],
                                        xh[c][:, 512 * j:512 * j + 512],
                                        start=(c == 0), stop=(c == NCH - 1))
                    return last
                em.op("T", chain,
                      reads=[("bwc", r)] + [("bxh", c) for c in range(NCH)],
                      writes=[pk(bk)])
                if dst is not None:
                    nmk = "bq" if dst is q else "bk"
                    em.op("V", lambda e, dst=dst, ot=ot, j=j, bk=bk: e.tensor_copy(
                        dst[ot][:, hb + 512 * j:hb + 512 * j + 512], P.bank(bk)),
                        reads=[pk(bk)], writes=[(nmk, (ot, None))])
                else:
                    em.op("V", lambda e, bk=bk: e.tensor_copy(vft[:], P.bank(bk)),
                          reads=[pk(bk)], writes=[("bvft", None)])
                    for i4 in range(4):
                        pb = i4 * 256
                        em.op("T", lambda e, i4=i4, pb=pb: e.transpose(
                            P.psb[:, pb:pb + 128], vft[:, 128 * i4:128 * i4 + 128],
                            g["ident_bf"][:]),
                            reads=[("bvft", None), ("ident_bf", None),
                                   ("psbg", None)],
                            writes=[("psb", i4), ("psbL", None)])
                    for i4 in range(4):
                        kb = 8 * half + 4 * j + i4
                        pb = i4 * 256
                        em.op("V", lambda e, kb=kb, ot=ot, pb=pb: e.tensor_copy(
                            v_tm[kb][:, 65 * 2 * ot:65 * 2 * ot + 64],
                            P.psb[:, pb:pb + 64]),
                            reads=[("psb", i4), ("psbL", None)],
                            writes=[("bvt", (kb, 2 * ot))])
                        em.op("V", lambda e, kb=kb, ot=ot, pb=pb: e.tensor_copy(
                            v_tm[kb][:, 65 * (2 * ot + 1):65 * (2 * ot + 1) + 64],
                            P.psb[:, pb + 64:pb + 128]),
                            reads=[("psb", i4), ("psbL", None)],
                            writes=[("bvt", (kb, 2 * ot + 1))]
                            + ([("psbg", None)] if i4 == 3 else []))
    P.pop()


def _attn_b_head(P, g, q, k, v_tm, out_b, bufs, h):
    em = P.em
    ch, hh = h // 2, h % 2
    Ebuf, frow, Fb, att, tbuf, sqa, rrow, rb, lamt, w64 = bufs
    for qt in range(4):
        kbs = list(range(0, 4 * qt + 4))
        for ki, kb in enumerate(kbs):
            sp = 2 + 2 * (ki % 2)
            for jj in range(2):
                po = 64 * hh + 32 * jj
                em.op("T", lambda e, ch=ch, po=po, kb=kb, qt=qt, sp=sp, jj=jj:
                      e.matmul(P.bank(sp + jj),
                               k[ch][po:po + 32, 128 * kb:128 * kb + 128],
                               q[ch][po:po + 32, 512 * qt:512 * qt + 512],
                               start=True, stop=True, tile_position=(po, 0)),
                      reads=[("bk", (ch, None)), ("bq", (ch, None))],
                      writes=[pk(sp + jj)])
            eb = Ebuf[ki % 2]
            em.op("S", lambda e, sp=sp, eb=eb: e.activation(
                out=eb[:], in_=P.ps[:, 512 * sp:512 * sp + 1024], func=AF.Exp,
                scale=ATTB_SCALE),
                reads=[pk(sp), pk(sp + 1)], writes=[("bE", None)])
            if kb >= 4 * qt:
                d = 128 * kb - 512 * qt
                em.op("V", lambda e, eb=eb, d=d: e.tensor_tensor(
                    out=eb[:].rearrange("p (a b) -> p a b", a=2),
                    in0=eb[:].rearrange("p (a b) -> p a b", a=2),
                    in1=g["tri"][:, 384 - d:384 - d + 512]
                    .rearrange("p (a b) -> p a b", a=1)
                    .broadcast_to([128, 2, 512]), op=ALU.mult),
                    reads=[("bE", None), ("tri", None)], writes=[("bE", None)])
            for jj in range(2):
                em.op("T", lambda e, kb=kb, h=h, jj=jj, eb=eb, st=(ki == 0),
                      sp2=(ki == len(kbs) - 1): e.matmul(
                    P.bank(jj, p=65), v_tm[kb][:, 65 * h:65 * h + 65],
                    eb[:, 512 * jj:512 * jj + 512], start=st, stop=sp2),
                    reads=[("bE", None), ("bvt", (kb, h)), ("bvt", (kb, "o"))],
                    writes=[pk(jj)])
        em.op("V", lambda e: e.reciprocal(frow[0:1, :], P.bank(0, p0=64, p=1)),
              reads=[pk(0)], writes=[("bfr", 0)])
        em.op("V", lambda e: e.reciprocal(frow[32:33, :], P.bank(1, p0=64, p=1)),
              reads=[pk(1)], writes=[("bfr", 1)])
        em.op("V", lambda e: e.tensor_scalar(
            out=frow[32:33, :], in0=frow[32:33, :], scalar1=lamt[32:33, :],
            scalar2=None, op0=ALU.mult),
            reads=[("bfr", 1), ("blam", None)], writes=[("bfr", 1)])
        em.op("T", lambda e: e.matmul(
            P.bank(4, p=64), g["ones_row_f"][0:1, 0:64], frow[0:1, :],
            start=True, stop=True),
            reads=[("bfr", 0), ("ones_row_f", None)], writes=[pk(4)])
        em.op("V", lambda e: e.tensor_copy(Fb[:], P.bank(4, p=64)),
              reads=[pk(4)], writes=[("bFb", None)])
        em.op("V", lambda e, qt=qt: e.tensor_tensor(
            out=att[:], in0=P.bank(0, p=64),
            in1=Fb[:], op=ALU.mult),
            reads=[pk(0), ("bFb", None)], writes=[("batt", None)])
        em.op("T", lambda e: e.matmul(
            P.bank(4, p=64), g["ones_row_f"][32:33, 0:64], frow[32:33, :],
            start=True, stop=True, tile_position=(32, 0)),
            reads=[("bfr", 1), ("ones_row_f", None), ("bFb", None)], writes=[pk(4)])
        em.op("V", lambda e: e.tensor_copy(Fb[:], P.bank(4, p=64)),
              reads=[pk(4)], writes=[("bFb", None)])
        em.op("V", lambda e: e.tensor_tensor(
            out=tbuf[:], in0=P.bank(1, p=64), in1=Fb[:], op=ALU.mult),
            reads=[pk(1), ("bFb", None)], writes=[("btb", None)])
        em.op("V", lambda e, qt=qt: e.tensor_tensor(
            out=att[:], in0=att[:], in1=tbuf[:], op=ALU.subtract),
            reads=[("batt", None), ("btb", None)], writes=[("batt", None)])
        em.op("S", lambda e: e.activation(out=sqa[:], in_=att[:], func=AF.Square),
              reads=[("batt", None)], writes=[("bsqa", None)])
        em.op("T", lambda e: e.matmul(
            P.bank(5, p0=0, p=1), g["ones_col_b"][0:64, :], sqa[:],
            start=True, stop=True),
            reads=[("bsqa", None), ("ones_col_b", None)], writes=[pk(5)])
        em.op("S", lambda e: e.activation(
            out=rrow[0:1, :], in_=P.bank(5, p0=0, p=1), func=AF.Sqrt,
            bias=g["epsr"][0:1, :], scale=1.0 / 64.0),
            reads=[pk(5), ("epsr", None), ("bfr", None)], writes=[("bfr", None)])
        em.op("V", lambda e: e.reciprocal(rrow[0:1, :], rrow[0:1, :]),
              reads=[("bfr", None)], writes=[("bfr", None)])
        em.op("T", lambda e: e.matmul(
            P.bank(6, p=64), g["ones_row_f"][0:1, 0:64], rrow[0:1, :],
            start=True, stop=True),
            reads=[("bfr", None), ("ones_row_f", None)], writes=[pk(6)])
        em.op("V", lambda e, ch=ch, hh=hh, qt=qt: e.scalar_tensor_tensor(
            out=out_b[ch][64 * hh:64 * hh + 64, 512 * qt:512 * qt + 512],
            in0=att[:], scalar=w64[:, 0:1],
            in1=P.bank(6, p=64), op0=ALU.mult, op1=ALU.mult),
            reads=[("batt", None), pk(6), ("bw64", None)],
            writes=[("outb", (ch, (hh, qt)))])


def attn_b(P, x, g):
    em, H = P.em, P.hbm
    P.push()
    q = [P.sbr(f"bq{c}", [128, T], BF16) for c in range(NCH)]
    k = [P.sbr(f"bk{c}", [128, T], BF16) for c in range(NCH)]
    v_tm = [P.sbr(f"bvt{t}", [128, 16 * 65], BF16) for t in range(16)]
    for kb in range(16):
        em.op("V", lambda e, kb=kb: e.memset(
            v_tm[kb][:].rearrange("p (h c) -> p h c", c=65)[:, :, 64:65], 1.0),
            writes=[("bvt", (kb, "o"))])
    for half in range(2):
        _attn_b_half(P, x, g, q, k, v_tm, half)

    out_b = [P.sbr(f"outb{c}", [128, T], BF16) for c in range(NCH)]
    P.push()
    lamt = P.sbr("blamt", [33, 1], F32)
    w64 = P.sbr("bw64", [64, 1], F32)
    P.push()
    lrow = P.sbr("blrow", [1, 128], F32)
    lscr = P.sbr("blscr", [1, 4], F32)
    for i, nm in enumerate(("lq1", "lk1", "lq2", "lk2")):
        em.op("SD", lambda e, i=i, nm=nm: e.dma_start(
            out=lrow[0:1, 32 * i:32 * i + 32], in_=H[nm]),
            writes=[("blrow", i)])
    em.op("V", lambda e: e.tensor_tensor(
        out=lscr[0:1, 0:1] if False else lrow[0:1, 0:32],
        in0=lrow[0:1, 0:32], in1=lrow[0:1, 32:64], op=ALU.mult),
        reads=[("blrow", None)], writes=[("blrow", "p1")])
    em.op("V", lambda e: e.tensor_reduce(
        out=lscr[0:1, 0:1], in_=lrow[0:1, 0:32], axis=AXX, op=ALU.add),
        reads=[("blrow", "p1")], writes=[("blscr", 0)])
    em.op("V", lambda e: e.tensor_tensor(
        out=lrow[0:1, 64:96], in0=lrow[0:1, 64:96], in1=lrow[0:1, 96:128],
        op=ALU.mult),
        reads=[("blrow", None)], writes=[("blrow", "p2")])
    em.op("V", lambda e: e.tensor_reduce(
        out=lscr[0:1, 1:2], in_=lrow[0:1, 64:96], axis=AXX, op=ALU.add),
        reads=[("blrow", "p2")], writes=[("blscr", 1)])
    em.op("S", lambda e: e.activation(
        out=lscr[0:1, 2:4], in_=lscr[0:1, 0:2], func=AF.Exp),
        reads=[("blscr", None)], writes=[("blscr", 2)])
    em.op("V", lambda e: e.tensor_tensor(
        out=lamt[0:1, :], in0=lscr[0:1, 2:3], in1=lscr[0:1, 3:4], op=ALU.subtract),
        reads=[("blscr", 2)], writes=[("blam", None)])
    em.op("V", lambda e: e.tensor_scalar(
        out=lamt[0:1, :], in0=lamt[0:1, :], scalar1=float(LAM_INIT), scalar2=None,
        op0=ALU.add),
        reads=[("blam", None)], writes=[("blam", None)])
    em.op("V", lambda e: e.tensor_copy(lamt[32:33, :], lamt[0:1, :]),
          reads=[("blam", None)], writes=[("blam", None)])
    P.pop()
    em.op("SD", lambda e: e.dma_start(
        out=w64[:, 0:1], in_=H["rms_w"].rearrange("(k p) -> p k", p=64),
        allow_slow_non_contiguous=True),
        writes=[("bw64", None)])
    em.op("V", lambda e: e.tensor_scalar(
        out=w64[:], in0=w64[:], scalar1=float(1.0 - LAM_INIT), scalar2=None,
        op0=ALU.mult),
        reads=[("bw64", None)], writes=[("bw64", None)])

    _bE = P.sbr("bE0", [128, 1024], BF16)
    Ebuf = [_bE, _bE]
    g["tri"] = P.sbr("btri", [128, 896], BF16)
    em.op("SD", lambda e: e.dma_start(out=g["tri"][:], in_=H["tri"]),
          writes=[("tri", None)])
    frow = P.sbr("bfrow", [33, 512], F32)
    Fb = P.sbr("bFbb", [64, 512], F32)
    att = P.sbr("batt", [64, 512], F32)
    tbuf = P.sbr("btbuf", [64, 512], F32)
    sqa = P.sbr("bsqa", [64, 512], BF16)
    rrow = frow
    rb = None
    bufs = (Ebuf, frow, Fb, att, tbuf, sqa, rrow, rb, lamt, w64)
    for h in range(16):
        _attn_b_head(P, g, q, k, v_tm, out_b, bufs, h)
    P.pop()

    P.push()
    wcol = [P.sbr(f"bwoc{i}", [128, 1024], BF16) for i in range(2)]
    for ot in range(8):
        r = ot % 2
        em.op("GD", lambda e, ot=ot, r=r: e.dma_start(
            out=wcol[r][:], in_=H["bWo"][:, 128 * ot:128 * ot + 128].rearrange(
                "(cc pp) c -> pp cc c", pp=128)),
            writes=[("bwoc", r)])
        for tt in range(4):
            bk = tt % 2
            def chain(e, r=r, tt=tt, bk=bk):
                last = None
                for c in range(NCH):
                    last = e.matmul(P.bank(bk), wcol[r][:, 128 * c:128 * c + 128],
                                    out_b[c][:, 512 * tt:512 * tt + 512],
                                    start=(c == 0), stop=(c == NCH - 1))
                return last
            em.op("T", chain,
                  reads=[("bwoc", r)] + [("outb", (c, None)) for c in range(NCH)],
                  writes=[pk(bk)])
            em.op("V", lambda e, ot=ot, tt=tt, bk=bk: e.scalar_tensor_tensor(
                out=x[ot][:, dsl(tt)], in0=P.bank(bk), scalar=0.0,
                in1=x[ot][:, dsl(tt)], op0=ALU.add, op1=ALU.add),
                reads=[pk(bk), ("x", (ot, None))], writes=[("x", (ot, None))])
    P.pop()
    P.pop()



def build_nc(upto=4):
    nc = bass.Bass("TRN2", target_bir_lowering=False, debug=False)
    H = {}
    H["x_in"] = nc.dram_tensor("x_in", [T, C], F32, kind="ExternalInput").ap()
    for nm in ("aWq", "aWk", "aWv", "aWo", "bWq", "bWk", "bWv", "bWo"):
        H[nm] = nc.dram_tensor(nm, [C, C], F32, kind="ExternalInput").ap()
    for nm in ("agq", "agk", "agv"):
        H[nm] = nc.dram_tensor(nm, [C], F32, kind="ExternalInput").ap()
    for nm in ("lq1", "lk1", "lq2", "lk2"):
        H[nm] = nc.dram_tensor(nm, [1, 32], F32, kind="ExternalInput").ap()
    H["rms_w"] = nc.dram_tensor("rms_w", [64], F32, kind="ExternalInput").ap()
    H["fwqkv"] = nc.dram_tensor("fwqkv", [2, C, 1536], F32, kind="ExternalInput").ap()
    H["fqs"] = nc.dram_tensor("fqs", [2, 64], F32, kind="ExternalInput").ap()
    H["fks"] = nc.dram_tensor("fks", [2, 64], F32, kind="ExternalInput").ap()
    H["fwo"] = nc.dram_tensor("fwo", [2, 512, C], F32, kind="ExternalInput").ap()
    H["flng"] = nc.dram_tensor("flng", [2, C], F32, kind="ExternalInput").ap()
    H["flnb"] = nc.dram_tensor("flnb", [2, C], F32, kind="ExternalInput").ap()
    H["fw1"] = nc.dram_tensor("fw1", [2, C, 5460], F32, kind="ExternalInput").ap()
    H["fb1"] = nc.dram_tensor("fb1", [2, 5460], F32, kind="ExternalInput").ap()
    H["fw2"] = nc.dram_tensor("fw2", [2, HID, C], F32, kind="ExternalInput").ap()
    H["fb2"] = nc.dram_tensor("fb2", [2, C], F32, kind="ExternalInput").ap()
    for nm, shp in (("ident_bf", [128, 128]), ("rot_csq", [128, T]),
                    ("rot_csk", [128, T]), ("rot_ssq", [128, T]),
                    ("rot_ssk", [128, T]), ("win_add", [128, 1408]),
                    ("win_mul", [128, 1408]), ("tri", [128, 896]),
                    ("rotp", [128, 128])):
        H[nm] = nc.dram_tensor(nm, shp, BF16, kind="ExternalInput").ap()
    H["ident_f32"] = nc.dram_tensor("ident_f32", [128, 128], F32,
                                    kind="ExternalInput").ap()
    H["e2f"] = nc.dram_tensor("e2f", [2, 128], F32, kind="ExternalInput").ap()
    H["e2b"] = nc.dram_tensor("e2b", [128, 2], BF16, kind="ExternalInput").ap()
    H["out"] = nc.dram_tensor("out", [T, C], F32, kind="ExternalOutput").ap()

    with ExitStack() as ctx:
        em = Em(nc, ctx)
        P = Prog(nc, ctx, em)
        P.hbm = H
        x = [P.sb(f"x{c}", [128, TP], F32) for c in range(NCH)]
        g = {}
        g["ident_bf"] = P.sb("identb", [128, 128], BF16)
        g["rotp"] = P.sb("rotp_sb", [128, 128], BF16)
        g["ones_row_f"] = P.sb("onesrf", [65, 128], F32)
        g["ones_row_b"] = P.sb("onesrb", [65, 128], BF16)
        g["ones_col_b"] = P.sb("onescb", [128, 1], BF16)
        g["e2f"] = P.sb("e2f", [2, 128], F32)
        g["e2b"] = P.sb("e2b", [128, 2], BF16)
        g["epsr"] = P.sb("epsr", [65, 1], F32)
        P.ps = ctx.enter_context(nc.psum_tensor("psA", [128, 7 * 512], F32))
        P.psb = ctx.enter_context(nc.psum_tensor("psBt", [128, 1024], BF16))

        p0_load(P, x, g)
        if upto >= 1:
            attn_a(P, x, g)
        if upto >= 2:
            ffn_call(P, x, g, 0)
        if upto >= 3:
            attn_b(P, x, g)
        if upto >= 4:
            ffn_call(P, x, g, 1)
        p5_store(P, x, g)

        with nc.Block() as block:
            em.render(block)
    return nc


def make_in_maps(inputs):
    tbl = host_tables()
    X = np.asarray(inputs["L0_attn_a_x"], np.float32)
    shared = {
        "aWq": np.ascontiguousarray(inputs["L0_attn_a_Wq"], dtype=np.float32),
        "aWk": np.ascontiguousarray(inputs["L0_attn_a_Wk"], dtype=np.float32),
        "aWv": np.ascontiguousarray(inputs["L0_attn_a_Wv"], dtype=np.float32),
        "aWo": np.ascontiguousarray(inputs["L0_attn_a_Wo"], dtype=np.float32),
        "agq": np.ascontiguousarray(np.asarray(inputs["L0_attn_a_gq"],
                                               np.float32).reshape(-1)),
        "agk": np.ascontiguousarray(np.asarray(inputs["L0_attn_a_gk"],
                                               np.float32).reshape(-1)),
        "agv": np.ascontiguousarray(np.asarray(inputs["L0_attn_a_gv"],
                                               np.float32).reshape(-1)),
        "bWq": np.ascontiguousarray(inputs["L0_attn_b_Wq"], dtype=np.float32),
        "bWk": np.ascontiguousarray(inputs["L0_attn_b_Wk"], dtype=np.float32),
        "bWv": np.ascontiguousarray(inputs["L0_attn_b_Wv"], dtype=np.float32),
        "bWo": np.ascontiguousarray(inputs["L0_attn_b_Wo"], dtype=np.float32),
        "lq1": np.ascontiguousarray(inputs["L0_attn_b_lq1"], dtype=np.float32).reshape(1, 32),
        "lk1": np.ascontiguousarray(inputs["L0_attn_b_lk1"], dtype=np.float32).reshape(1, 32),
        "lq2": np.ascontiguousarray(inputs["L0_attn_b_lq2"], dtype=np.float32).reshape(1, 32),
        "lk2": np.ascontiguousarray(inputs["L0_attn_b_lk2"], dtype=np.float32).reshape(1, 32),
        "rms_w": np.ascontiguousarray(inputs["L0_attn_b_rms_w"], dtype=np.float32),
        "fwqkv": np.ascontiguousarray(inputs["L0_ffn_wqkv"], dtype=np.float32),
        "fqs": np.ascontiguousarray(inputs["L0_ffn_q_scale"], dtype=np.float32),
        "fks": np.ascontiguousarray(inputs["L0_ffn_k_scale"], dtype=np.float32),
        "fwo": np.ascontiguousarray(inputs["L0_ffn_wo"], dtype=np.float32),
        "flng": np.ascontiguousarray(inputs["L0_ffn_ln_g"], dtype=np.float32),
        "flnb": np.ascontiguousarray(inputs["L0_ffn_ln_b"], dtype=np.float32),
        "fw1": np.ascontiguousarray(inputs["L0_ffn_w1"], dtype=np.float32),
        "fb1": np.ascontiguousarray(inputs["L0_ffn_b1"], dtype=np.float32),
        "fw2": np.ascontiguousarray(inputs["L0_ffn_w2"], dtype=np.float32),
        "fb2": np.ascontiguousarray(inputs["L0_ffn_b2"], dtype=np.float32),
    }
    shared.update(tbl)
    B = X.shape[0]
    return [dict(shared, x_in=np.ascontiguousarray(X[b])) for b in range(B)]


_NC = {}

def kernel(**inputs):
    if "nc" not in _NC:
        _NC["nc"] = build_nc(upto=4)
    in_maps = make_in_maps(inputs)
    res = run_bass_kernel_spmd(_NC["nc"], in_maps, core_ids=list(range(len(in_maps))))
    return np.stack([r["out"] for r in res.results], axis=0).astype(np.float32)


# version 04
# speedup vs baseline: 1.0057x; 1.0057x over previous
import math
import numpy as np
import ml_dtypes
from contextlib import ExitStack

import concourse.bass as bass
import concourse.mybir as mybir
from concourse.bass_utils import run_bass_kernel_spmd

F32 = mybir.dt.float32
BF16 = mybir.dt.bfloat16
AF = mybir.ActivationFunctionType
ALU = mybir.AluOpType
AXX = mybir.AxisListType.X

T = 2048
TP = T + 1
C = 1024
NCH = 8
LAM_INIT = 0.8 - 0.6 * math.exp(-0.3 * 12)
ATTB_SCALE = 32.0 ** -0.5
HID = 2730
DEDUP_WAITS = True
DBG = {"keep": False, "names": {}}


class Em:
    NDMA = 8
    PHYS = {"T": "tensor", "V": "vector", "S": "scalar", "G": "gpsimd"}
    for _i in range(NDMA):
        PHYS[f"GD{_i}"] = "gpsimd"
        PHYS[f"SD{_i}"] = "sync"
    ORDER = ["tensor", "vector", "scalar", "gpsimd", "sync"]
    DMA = tuple(f"{q}{i}" for q in ("GD", "SD") for i in range(8))

    def __init__(self, nc, ctx):
        self.nc = nc
        self.sems = {v: ctx.enter_context(nc.semaphore("sem" + v)) for v in self.PHYS}
        self.ops = {p: [] for p in self.ORDER}
        self.seq = {v: 0 for v in self.PHYS}
        self.pending = {v: {} for v in self.PHYS}
        self.known = {v: {w: 0 for w in self.PHYS} for v in self.PHYS}
        self.lastw = {}
        self.readers = {}
        self.targets = {v: set() for v in self.PHYS}
        self.rr = {"GD": 0, "SD": 0}

    @staticmethod
    def _conf(k1, k2):
        n1, n2 = str(k1[0]), str(k2[0])
        if PSUM_ONEKEY == 1 and n1.startswith("ps") and n2.startswith("ps"):
            return True
        if PSUM_ONEKEY == 2:
            b1 = n1 == "psb" or n1 == "psbL" or n1 == "psbg"
            b2 = n2 == "psb" or n2 == "psbL" or n2 == "psbg"
            a1 = n1.startswith("ps") and not b1
            a2 = n2.startswith("ps") and not b2
            if (b1 and a2) or (b2 and a1):
                return True
        return k1[0] == k2[0] and (k1[1] is None or k2[1] is None or k1[1] == k2[1])

    def op(self, v, fn, reads=(), writes=()):
        if v in ("GD", "SD"):
            q = v
            v = f"{q}{self.rr[q] % self.NDMA}"
            self.rr[q] += 1
        deps = {}

        def dep(pv, pseq):
            if pv == v and v == "T":
                return
            deps[pv] = max(deps.get(pv, 0), pseq)

        for key in reads:
            for k2, (pv, pseq) in self.lastw.items():
                if self._conf(key, k2):
                    dep(pv, pseq)
        for key in writes:
            for k2, (pv, pseq) in self.lastw.items():
                if self._conf(key, k2):
                    dep(pv, pseq)
            for k2, rd in self.readers.items():
                if self._conf(key, k2):
                    for pv, pseq in rd.items():
                        dep(pv, pseq)
        for pv, pseq in self.pending[v].items():
            deps[pv] = max(deps.get(pv, 0), pseq)
        self.pending[v] = {}
        if v in self.DMA and self.seq[v] > 0:
            deps[v] = max(deps.get(v, 0), self.seq[v])
        wl = []
        for pv, pseq in deps.items():
            if pv == v:
                wl.append((pv, pseq))
                self.targets[pv].add(pseq)
            else:
                if DEDUP_WAITS and pseq <= self.known[v][pv]:
                    continue
                wl.append((pv, pseq))
                self.targets[pv].add(pseq)
                self.known[v][pv] = max(self.known[v][pv], pseq)
        step = 16 if v in self.DMA else 1
        self.seq[v] += step
        myseq = self.seq[v]
        for key in writes:
            self.lastw[key] = (v, myseq)
            self.readers.pop(key, None)
        for key in reads:
            self.readers.setdefault(key, {})[v] = myseq
        self.ops[self.PHYS[v]].append((wl, v, myseq, fn))

    def barrier(self):
        for v in self.PHYS:
            for w in self.PHYS:
                if w != v and self.seq[w] > (self.known[v][w] if DEDUP_WAITS else 0):
                    self.pending[v][w] = max(self.pending[v].get(w, 0), self.seq[w])
                    self.targets[w].add(self.seq[w])
        self.lastw.clear()
        self.readers.clear()

    def render(self, block):
        incval = {}
        for v in self.PHYS:
            if v not in self.DMA:
                incval[v] = {s: i + 1 for i, s in enumerate(sorted(self.targets[v]))}

        def val(pv, pseq):
            if pv in self.DMA:
                return pseq
            m = incval[pv]
            if pseq in m:
                return m[pseq]
            return m[min(s for s in m if s >= pseq)]

        def make_body(phys, oplist, is_sync):
            def body(eng):
                for wl, v, myseq, fn in oplist:
                    for pv, pseq in wl:
                        eng.wait_ge(self.sems[pv], val(pv, pseq))
                    inst = fn(eng)
                    if v in self.DMA:
                        inst.then_inc(self.sems[v], 16)
                    elif myseq in incval[v]:
                        inst.then_inc(self.sems[v], 1)
                if is_sync:
                    for v in self.DMA:
                        if self.seq[v] > 0:
                            eng.wait_ge(self.sems[v], self.seq[v])
            return body

        for phys in self.ORDER:
            oplist = self.ops[phys]
            if not oplist and phys != "sync":
                continue
            getattr(block, phys)(make_body(phys, oplist, phys == "sync"))


PSUM_ONEKEY = False

def pk(b, sub=None):
    if PSUM_ONEKEY:
        return ("psALL", None)
    return (f"ps{b}", sub)



def _host_tables():
    t = {}
    t["ident_bf"] = np.eye(128).astype(ml_dtypes.bfloat16)
    t["ident_f32"] = np.eye(128, dtype=np.float32)
    half = np.arange(0, 64, 2, dtype=np.float32)
    inv_freq = (1.0 / 10000.0 ** (half / 64.0)).astype(np.float32)
    tt = np.arange(T, dtype=np.float32)
    freqs = np.concatenate([tt[:, None] * inv_freq[None, :]] * 2, axis=1)
    base = ((half + 0.4 * 64) / (1.4 * 64)).astype(np.float32)
    power = (tt - T // 2) / 256.0
    scale = np.concatenate([base[None, :] ** power[:, None]] * 2, axis=1)
    cos, sin = np.cos(freqs), np.sin(freqs)
    t["rot_csq"] = np.concatenate([(cos * scale).T] * 2, 0).astype(np.float32)
    t["rot_csk"] = np.concatenate([(cos / scale).T] * 2, 0).astype(np.float32)
    t["rot_ssq"] = np.concatenate([(sin * scale).T] * 2, 0).astype(np.float32)
    t["rot_ssk"] = np.concatenate([(sin / scale).T] * 2, 0).astype(np.float32)
    rp = np.zeros((128, 128), np.float32)
    for blk in range(2):
        b = 64 * blk
        for d in range(32):
            rp[b + d + 32, b + d] = -1.0
            rp[b + d, b + d + 32] = 1.0
    t["rotp"] = rp.astype(ml_dtypes.bfloat16)
    kk = np.arange(128)[:, None]
    cc = np.arange(1408)[None, :]
    allowed = (cc >= kk + 384) & (cc <= kk + 896)
    t["win_add"] = np.where(allowed, 0.0, -30000.0).astype(ml_dtypes.bfloat16)
    t["win_mul"] = allowed.astype(ml_dtypes.bfloat16)
    cc2 = np.arange(896)[None, :]
    t["tri"] = (cc2 >= kk + 384).astype(ml_dtypes.bfloat16)
    e2 = np.zeros((2, 128), np.float32)
    e2[0, 0:64] = 1.0
    e2[1, 64:128] = 1.0
    t["e2f"] = e2
    t["e2b"] = np.ascontiguousarray(e2.T).astype(ml_dtypes.bfloat16)
    return t

_TBL = None

def host_tables():
    global _TBL
    if _TBL is None:
        _TBL = _host_tables()
    return _TBL



class Prog:
    def __init__(self, nc, ctx, em):
        self.nc = nc
        self.ctx = ctx
        self.em = em
        self.hbm = {}
        self._stacks = []

    _uid = 0

    def _unm(self, name):
        Prog._uid += 1
        return f"{name}_u{Prog._uid}"

    def sb(self, name, shape, dt, side="left"):
        return self.ctx.enter_context(
            self.nc.sbuf_tensor(self._unm(name), shape, dt, side=side, unsafe_allow_realloc=True))

    def push(self):
        self._stacks.append(ExitStack())

    def pop(self):
        self._stacks.pop().close()
        self.em.barrier()

    def sbr(self, name, shape, dt):
        return self._stacks[-1].enter_context(
            self.nc.sbuf_tensor(self._unm(name), shape, dt, side="right", unsafe_allow_realloc=True))

    def bank(self, b, w=512, p0=0, p=128, off=0):
        return self.ps[p0:p0 + p, 512 * b + off:512 * b + off + w]


def dsl(j, w=512):
    return slice(1 + w * j, 1 + w * j + w)



def ln_half(P, g, rows, half, src, tag):
    em = P.em
    arow, brow, srow = rows
    sqt = g[f"{tag}sqt"]
    for c in range(NCH):
        for j in range(2):
            rk, ap = src(c, j)
            st, sp = (c == 0), (c == NCH - 1)
            i = (2 * c + j) % 2
            em.op("S", lambda e, ap=ap, i=i: e.activation(
                out=sqt[i][:], in_=ap, func=AF.Square),
                reads=rk, writes=[(f"{tag}sqt", i)])
            em.op("T", lambda e, ap=ap, j=j, st=st, sp=sp: e.matmul(
                P.bank(3 + j, p0=0, p=1), g["ones_col_b"][:], ap, start=st, stop=sp),
                reads=rk + [("ones_col_b", None)], writes=[pk(3 + j, "r0")])
            em.op("T", lambda e, j=j, i=i, st=st, sp=sp: e.matmul(
                P.bank(5 + j, p0=0, p=1), g["ones_col_b"][:], sqt[i][:],
                start=st, stop=sp),
                reads=[(f"{tag}sqt", i), ("ones_col_b", None)], writes=[pk(5 + j, "r1")])
    for j in range(2):
        sl = slice(512 * j, 512 * j + 512)
        em.op("V", lambda e, j=j, sl=sl: e.tensor_scalar(
            out=brow[0:1, sl], in0=P.bank(3 + j, p0=0, p=1), scalar1=1.0 / C,
            scalar2=None, op0=ALU.mult),
            reads=[pk(3 + j, "r0")], writes=[("lnrows", ("m", j))])
        em.op("V", lambda e, sl=sl: e.tensor_tensor(
            out=srow[0:1, sl], in0=brow[0:1, sl], in1=brow[0:1, sl], op=ALU.mult),
            reads=[("lnrows", ("m", j))], writes=[("lnrows", ("s", j))])
        em.op("V", lambda e, j=j, sl=sl: e.scalar_tensor_tensor(
            out=srow[0:1, sl], in0=P.bank(5 + j, p0=0, p=1), scalar=1.0 / C,
            in1=srow[0:1, sl], op0=ALU.mult, op1=ALU.subtract),
            reads=[pk(5 + j, "r1"), ("lnrows", ("s", j))], writes=[("lnrows", ("s", j))])
        em.op("S", lambda e, sl=sl: e.activation(
            out=srow[0:1, sl], in_=srow[0:1, sl], func=AF.Sqrt,
            bias=g["epsr"][0:1, :], scale=1.0),
            reads=[("lnrows", ("s", j)), ("epsr", None)],
            writes=[("lnrows", ("s", j))])
        em.op("V", lambda e, sl=sl: e.reciprocal(arow[0:1, sl], srow[0:1, sl]),
            reads=[("lnrows", ("s", j))], writes=[("lnrows", ("a", j))])
        em.op("V", lambda e, sl=sl: e.scalar_tensor_tensor(
            out=brow[0:1, sl], in0=brow[0:1, sl], scalar=-1.0, in1=arow[0:1, sl],
            op0=ALU.mult, op1=ALU.mult),
            reads=[("lnrows", ("a", j)), ("lnrows", ("m", j))],
            writes=[("lnrows", ("b", j))])


def ln_bcast(P, g, rows, j, bR, bB):
    em = P.em
    arow, brow, srow = rows
    sl = slice(512 * j, 512 * j + 512)
    em.op("T", lambda e: e.matmul(
        P.bank(bR), g["ones_row_f"][0:1, :], arow[0:1, sl], start=True, stop=True),
        reads=[("lnrows", ("a", j)), ("ones_row_f", None)], writes=[pk(bR)])
    em.op("T", lambda e: e.matmul(
        P.bank(bB), g["ones_row_f"][0:1, :], brow[0:1, sl], start=True, stop=True),
        reads=[("lnrows", ("b", j)), ("ones_row_f", None)], writes=[pk(bB)])



def p0_load(P, x, g):
    em, H = P.em, P.hbm
    P.push()
    idf = P.sbr("idf0", [128, 128], F32)
    stage = [P.sbr(f"stg{i}", [128, C], F32) for i in range(2)]
    for nm in ("ident_bf", "rotp"):
        em.op("SD", lambda e, nm=nm: e.dma_start(out=g[nm][:], in_=H[nm]),
              writes=[(nm, None)])
    em.op("SD", lambda e: e.dma_start(out=idf[:], in_=H["ident_f32"]),
          writes=[("idf0", None)])
    em.op("V", lambda e: e.memset(g["ones_row_f"][:], 1.0), writes=[("ones_row_f", None)])
    em.op("V", lambda e: e.memset(g["ones_row_b"][:], 1.0), writes=[("ones_row_b", None)])
    em.op("V", lambda e: e.memset(g["ones_col_b"][:], 1.0), writes=[("ones_col_b", None)])
    em.op("V", lambda e: e.memset(g["epsr"][:], 1e-5), writes=[("epsr", None)])
    em.op("SD", lambda e: e.dma_start(out=g["e2f"][:], in_=H["e2f"]),
          writes=[("e2f", None)])
    em.op("SD", lambda e: e.dma_start(out=g["e2b"][:], in_=H["e2b"]),
          writes=[("e2b", None)])
    for c in range(NCH):
        em.op("V", lambda e, c=c: e.memset(x[c][:, 0:1], 0.0), writes=[("x", (c, "p"))])
    if DBG.get("skip_p0"):
        for c in range(NCH):
            em.op("V", lambda e, c=c: e.memset(x[c][:, 1:TP], 0.25),
                  writes=[("x", (c, None))])
        P.pop()
        return
    for tt in range(16):
        st = stage[tt % 2]
        em.op("SD", lambda e, tt=tt, st=st: e.dma_start(
            out=st[:], in_=H["x_in"][128 * tt:128 * tt + 128, :]),
            writes=[("stg", tt % 2)])
        for c in range(NCH):
            bk = c % 3
            em.op("T", lambda e, st=st, c=c, bk=bk: e.transpose(
                P.bank(bk, 128), st[:, 128 * c:128 * c + 128], idf[:]),
                reads=[("stg", tt % 2), ("idf0", None)], writes=[pk(bk)])
            em.op("V", lambda e, c=c, tt=tt, bk=bk: e.tensor_copy(
                x[c][:, 1 + 128 * tt:1 + 128 * tt + 128], P.bank(bk, 128)),
                reads=[pk(bk)], writes=[("x", (c, tt // 4))])
    P.pop()


def p5_store(P, x, g):
    em, H = P.em, P.hbm
    if DBG.get("skip_p5"):
        for tt in range(16):
            em.op("SD", lambda e, tt=tt: e.dma_start(
                out=H["out"][128 * tt:128 * tt + 128, :],
                in_=x[tt % NCH][:, 1:1 + C]),
                reads=[("x", (tt % NCH, None))], writes=[("outhbm", tt)])
        return
    P.push()
    idf = P.sbr("idf5", [128, 128], F32)
    ostage = [P.sbr(f"ost{i}", [128, C], F32) for i in range(2)]
    em.op("SD", lambda e: e.dma_start(out=idf[:], in_=H["ident_f32"]),
          writes=[("idf5", None)])
    for tt in range(16):
        st = ostage[tt % 2]
        for c in range(NCH):
            bk = c % 3
            em.op("T", lambda e, c=c, tt=tt, bk=bk: e.transpose(
                P.bank(bk, 128), x[c][:, 1 + 128 * tt:1 + 128 * tt + 128], idf[:]),
                reads=[("x", (c, None)), ("idf5", None)], writes=[pk(bk)])
            em.op("V", lambda e, st=st, c=c, bk=bk, tt=tt: e.tensor_copy(
                st[:, 128 * c:128 * c + 128], P.bank(bk, 128)),
                reads=[pk(bk)], writes=[("ost", tt % 2)])
        em.op("SD", lambda e, st=st, tt=tt: e.dma_start(
            out=H["out"][128 * tt:128 * tt + 128, :], in_=st[:]),
            reads=[("ost", tt % 2)], writes=[("outhbm", tt)])
    P.pop()



def _attn_a_half(P, x, g, q, k, v_tm, half, projstop=None):
    em, H = P.em, P.hbm
    hb = 1024 * half
    P.push()
    xa = [P.sbr(f"xah{c}", [128, 1024], BF16) for c in range(NCH)]
    skipxa = DBG.get("aa_skipxa")
    P.push()
    rows = (P.sbr("lnraa", [1, 1024], F32), P.sbr("lnrab", [1, 1024], F32), P.sbr("lnras", [1, 1024], F32))
    g["a_sqt"] = [P.sbr(f"asqt{i}", [128, 512], BF16) for i in range(2)]
    xbt = [P.sbr(f"axbt{i}", [128, 512], BF16) for i in range(2)]
    tmp = [P.sbr(f"atmp{i}", [128, 512], F32) for i in range(2)]

    def src(c, j):
        i = (2 * c + j) % 2
        em.op("V", lambda e, c=c, j=j, i=i: e.tensor_copy(
            xbt[i][:], x[c][:, 1 + hb + 512 * j:1 + hb + 512 * j + 512]),
            reads=[("x", (c, None))], writes=[("axbt", i)])
        return [("axbt", i)], xbt[i][:]
    if not skipxa:
        ln_half(P, g, rows, half, src, "a_")
        for j in range(2):
            ln_bcast(P, g, rows, j, 0, 1)
            for c in range(NCH):
                i = (2 * c + j) % 2
                slp = slice(1 + hb + 512 * j, 1 + hb + 512 * j + 512)
                em.op("V", lambda e, c=c, i=i, slp=slp: e.tensor_tensor(
                    out=tmp[i][:], in0=x[c][:, slp], in1=P.bank(0), op=ALU.mult),
                    reads=[("x", (c, None)), pk(0)], writes=[("atmp", i)])
                em.op("V", lambda e, i=i: e.tensor_tensor(
                    out=tmp[i][:], in0=tmp[i][:], in1=P.bank(1), op=ALU.add),
                    reads=[("atmp", i), pk(1)], writes=[("atmp", i)])
                em.op("S", lambda e, c=c, j=j, i=i: e.activation(
                    out=xa[c][:, 512 * j:512 * j + 512], in_=tmp[i][:], func=AF.Tanh),
                    reads=[("atmp", i)], writes=[("xah", (c, j))])
    else:
        for c in range(NCH):
            em.op("V", lambda e, c=c: e.memset(xa[c][:], 0.125),
                  writes=[("xah", (c, None))])
    P.pop()

    if projstop == 0:
        P.pop()
        return
    P.push()
    xsq_hi = P.sbr("xsqh", [1, 1024], BF16)
    xsq_lo = P.sbr("xsql", [1, 1024], BF16)
    sq2 = [P.sbr(f"asq2{i}", [128, 512], BF16) for i in range(2)]
    wrow = P.sbr("wrow", [128, C], F32)
    wtan = P.sbr("wtan", [128, C], BF16)
    wsqs = P.sbr("wsqs", [128, 512], BF16)
    wsqh = P.sbr("wsqh", [128, 2], F32)
    wsq = P.sbr("wsqc", [128, 1], F32)
    waT = P.sbr("waT", [128, C], BF16)
    vft = P.sbr("vft", [128, 512], BF16)
    gcol = P.sbr("gcol", [128, 8], F32)
    s1c = P.sbr("s1c", [128, 8], F32)
    s2c = P.sbr("s2c", [128, 8], F32)
    skipxsq = DBG.get("aa_skipxsq")
    if skipxsq:
        em.op("V", lambda e: e.memset(xsq_hi[:], 0.5), writes=[("xsq", ("h", 0)), ("xsq", ("h", 1))])
        em.op("V", lambda e: e.memset(xsq_lo[:], 0.0), writes=[("xsq", ("l", 0)), ("xsq", ("l", 1))])
    for c in range(NCH if not skipxsq else 0):
        for j in range(2):
            i = (2 * c + j) % 2
            em.op("S", lambda e, c=c, j=j, i=i: e.activation(
                out=sq2[i][:], in_=xa[c][:, 512 * j:512 * j + 512], func=AF.Square),
                reads=[("xah", (c, j))], writes=[("asq2", i)])
            em.op("T", lambda e, j=j, i=i, c=c: e.matmul(
                P.bank(3 + j, p0=0, p=1), g["ones_col_b"][:], sq2[i][:],
                start=(c == 0), stop=(c == NCH - 1)),
                reads=[("asq2", i), ("ones_col_b", None)], writes=[pk(3 + j, "r0")])
    for j in range(2 if not skipxsq else 0):
        sl = slice(512 * j, 512 * j + 512)
        em.op("V", lambda e, j=j, sl=sl: e.tensor_copy(
            xsq_hi[0:1, sl], P.bank(3 + j, p0=0, p=1)),
            reads=[pk(3 + j, "r0")], writes=[("xsq", ("h", j))])
        em.op("V", lambda e, j=j, sl=sl: e.tensor_tensor(
            out=xsq_lo[0:1, sl], in0=P.bank(3 + j, p0=0, p=1), in1=xsq_hi[0:1, sl],
            op=ALU.subtract),
            reads=[pk(3 + j, "r0"), ("xsq", ("h", j))], writes=[("xsq", ("l", j))])

    for pi_, (nm, wnm, gnm, fn) in enumerate((("aq", "aWq", "agq", AF.Tanh),
                             ("ak", "aWk", "agk", AF.Tanh),
                             ("av", "aWv", "agv", AF.Identity))):
        if projstop is not None and pi_ >= projstop - 1:
            break
        dst = {"aq": q, "ak": k, "av": None}[nm]
        em.op("SD", lambda e, gnm=gnm: e.dma_start(
            out=gcol[:], in_=H[gnm].rearrange("(k p) -> p k", p=128),
            allow_slow_non_contiguous=True),
            writes=[("gcol", None)])
        em.op("V", lambda e: e.tensor_scalar(
            out=s1c[:], in0=gcol[:], scalar1=-1.0 / C, scalar2=None, op0=ALU.mult),
            reads=[("gcol", None)], writes=[("s1c", None)])
        em.op("V", lambda e: e.tensor_scalar(
            out=s2c[:], in0=gcol[:], scalar1=0.5, scalar2=None, op0=ALU.mult),
            reads=[("gcol", None)], writes=[("s2c", None)])
        psub = DBG.get("aa_projsub")
        for ot in range(DBG.get("aa_nots", 8)):
            em.op("SD", lambda e, wnm=wnm, ot=ot: e.dma_start(
                out=wrow[:], in_=H[wnm][128 * ot:128 * ot + 128, :]),
                writes=[("wrow", None)])
            em.op("S", lambda e: e.activation(
                out=wtan[:], in_=wrow[:], func=AF.Tanh),
                reads=[("wrow", None)], writes=[("wtan", None)])
            for hf in range(2):
                em.op("S", lambda e, hf=hf: e.activation(
                    out=wsqs[:], in_=wtan[:, 512 * hf:512 * hf + 512],
                    func=AF.Square, accum_out=wsqh[:, hf:hf + 1]),
                    reads=[("wtan", None)],
                    writes=[("wsqs", None), ("wsqh", hf)])
            em.op("V", lambda e: e.tensor_tensor(
                out=wsq[:], in0=wsqh[:, 0:1], in1=wsqh[:, 1:2], op=ALU.add),
                reads=[("wsqh", 0), ("wsqh", 1)], writes=[("wsqc", None)])
            if psub is not None and psub < 2:
                continue
            em.op("V", lambda e: e.tensor_scalar(
                out=wtan[:], in0=wtan[:], scalar1=-2.0, scalar2=None, op0=ALU.mult),
                reads=[("wtan", None)], writes=[("wtan", None)])
            if psub is not None and psub < 3:
                continue
            for grp in range(DBG.get("aa_ngrp", 2)):
                for c4 in range(DBG.get("aa_ntr", 4)):
                    c = 4 * grp + c4
                    pb = c4 * 256
                    em.op("T", lambda e, c=c, pb=pb: e.transpose(
                        P.psb[:, pb:pb + 128], wtan[:, 128 * c:128 * c + 128],
                        g["ident_bf"][:]),
                        reads=[("wtan", None), ("ident_bf", None), ("psbg", None)],
                        writes=[("psb", c4), ("psbL", None)])
                for c4 in range(DBG.get("aa_ntr", 4)):
                    c = 4 * grp + c4
                    pb = c4 * 256
                    em.op("V", lambda e, c=c, pb=pb: e.tensor_copy(
                        waT[:, 128 * c:128 * c + 128], P.psb[:, pb:pb + 128]),
                        reads=[("psb", c4), ("psbL", None)],
                        writes=[("waT", c)]
                        + ([("psbg", None)] if c4 == DBG.get("aa_ntr", 4) - 1 else []))
            if psub is not None and psub < 4:
                continue
            for j in range(2):
                bk = j
                sl = slice(512 * j, 512 * j + 512)
                slp = slice(1 + hb + 512 * j, 1 + hb + 512 * j + 512)

                def chain(e, sl=sl, bk=bk, xh_=xsq_hi, xl_=xsq_lo):
                    for c in range(NCH):
                        e.matmul(P.bank(bk), waT[:, 128 * c:128 * c + 128],
                                 xa[c][:, sl], start=(c == 0), stop=False)
                    e.matmul(P.bank(bk), g["ones_row_b"][0:1, :], xh_[0:1, sl],
                             start=False, stop=False)
                    return e.matmul(P.bank(bk), g["ones_row_b"][0:1, :],
                                    xl_[0:1, sl], start=False, stop=True)
                em.op("T", chain,
                      reads=[("waT", c) for c in range(NCH)]
                      + [("xah", (c, None)) for c in range(NCH)]
                      + [("xsq", None), ("ones_row_b", None)],
                      writes=[pk(bk)])
                if psub is not None and psub < 5:
                    continue
                em.op("V", lambda e, bk=bk: e.tensor_scalar(
                    out=P.bank(bk), in0=P.bank(bk), scalar1=wsq[:], scalar2=0.0,
                    op0=ALU.add, op1=ALU.max),
                    reads=[pk(bk), ("wsqc", None)], writes=[pk(bk)])
                if psub is not None and psub < 6:
                    continue
                if dst is not None:
                    em.op("S", lambda e, dst=dst, ot=ot, slp=slp, bk=bk, fn=fn:
                          e.activation(out=dst[ot][:, slp], in_=P.bank(bk),
                                       func=fn, bias=s2c[:, ot:ot + 1],
                                       scale=s1c[:, ot:ot + 1]),
                          reads=[pk(bk), ("s1c", None), ("s2c", None)],
                          writes=[(nm, (ot, half * 2 + j))])
                elif psub == 9:
                    em.op("V", lambda e: e.memset(vft[:], 0.5),
                          reads=[], writes=[("vft", None)])
                elif psub is None or psub >= 7:
                    em.op("V", lambda e, ot=ot, bk=bk: e.tensor_scalar(
                        out=vft[:], in0=P.bank(bk), scalar1=s1c[:, ot:ot + 1],
                        scalar2=s2c[:, ot:ot + 1], op0=ALU.mult, op1=ALU.add),
                        reads=[pk(bk), ("s1c", None), ("s2c", None)],
                        writes=[("vft", None)])
                    for i4 in range(4):
                        kb = 8 * half + 4 * j + i4
                        pb = (i4 % 4) * 256
                        em.op("T", lambda e, i4=i4, pb=pb: e.transpose(
                            P.psb[:, pb:pb + 128],
                            vft[:, 128 * i4:128 * i4 + 128], g["ident_bf"][:]),
                            reads=[("vft", None), ("ident_bf", None),
                                   ("psbg", None)],
                            writes=[("psb", i4 % 4), ("psbL", None)])
                        em.op("V", lambda e, kb=kb, ot=ot, pb=pb: e.tensor_copy(
                            v_tm[kb][:, 65 * 2 * ot:65 * 2 * ot + 64],
                            P.psb[:, pb:pb + 64]),
                            reads=[("psb", i4 % 4), ("psbL", None)],
                            writes=[("avt", (kb, 2 * ot))])
                        em.op("V", lambda e, kb=kb, ot=ot, pb=pb: e.tensor_copy(
                            v_tm[kb][:, 65 * (2 * ot + 1):65 * (2 * ot + 1) + 64],
                            P.psb[:, pb + 64:pb + 128]),
                            reads=[("psb", i4 % 4), ("psbL", None)],
                            writes=[("avt", (kb, 2 * ot + 1))]
                            + ([("psbg", None)] if i4 == 3 else []))
    P.pop()
    P.pop()



def attn_a(P, x, g):
    em, H = P.em, P.hbm
    P.push()
    q = [P.sbr(f"aq{c}", [128, TP], BF16) for c in range(NCH)]
    k = [P.sbr(f"ak{c}", [128, TP], BF16) for c in range(NCH)]
    v_tm = [P.sbr(f"avt{t}", [128, 16 * 65], BF16) for t in range(16)]

    for c in range(NCH):
        em.op("V", lambda e, c=c: e.memset(q[c][:, 0:1], 0.0), writes=[("aq", (c, "p"))])
        em.op("V", lambda e, c=c: e.memset(k[c][:, 0:1], 0.0), writes=[("ak", (c, "p"))])
    for kb in range(16):
        em.op("V", lambda e, kb=kb: e.memset(
            v_tm[kb][:].rearrange("p (h c) -> p h c", c=65)[:, :, 64:65], 1.0),
            writes=[("avt", (kb, "ones"))])

    aa = DBG.get("aa_stop")
    for half in range(2):
        if aa is None or aa >= 2 or (aa == 1 and half == 0):
            _attn_a_half(P, x, g, q, k, v_tm, half, DBG.get("aa_projstop"))

    if DBG.get("aa_stop") is not None and DBG["aa_stop"] < 3:
        P.pop()
        return
    out_a = [P.sbr(f"oa{c}", [128, T], BF16) for c in range(NCH)]
    P.push()
    ksqT = P.sbr("ksqT", [128, 256], F32)
    ksq_sq = [P.sbr("kss0", [128, 512], BF16)] * 2
    Ebuf = [P.sbr(f"aE{i}", [128, 1024], BF16) for i in range(2)]
    rrow = P.sbr("arr", [1, 512], F32)
    rb = P.sbr("arb", [64, 512], F32)
    for c in range(NCH):
        for tt in range(4):
            i = (4 * c + tt) % 2
            em.op("S", lambda e, c=c, tt=tt, i=i: e.activation(
                out=ksq_sq[i][:], in_=k[c][:, dsl(tt)], func=AF.Square),
                reads=[("ak", (c, None))], writes=[("kss", None)])
            for j4 in range(4):
                kb = 4 * tt + j4
                for hh in range(2):
                    h = 2 * c + hh
                    em.op("T", lambda e, i=i, j4=j4, hh=hh, h=h, kb=kb: e.matmul(
                        P.bank(6, w=1, off=16 * h + kb),
                        ksq_sq[i][64 * hh:64 * hh + 64, 128 * j4:128 * j4 + 128],
                        g["ones_col_b"][64 * hh:64 * hh + 64, :],
                        start=True, stop=True, tile_position=(64 * hh, 0)),
                        reads=[("kss", None), ("ones_col_b", None)],
                        writes=[pk(6, 16 * h + kb)])
    em.op("V", lambda e: e.tensor_scalar(
        out=ksqT[:], in0=P.bank(6, w=256), scalar1=-1.0 / 80.0, scalar2=None,
        op0=ALU.mult),
        reads=[pk(6, None)], writes=[("ksqT", None)])

    for h in range(16):
        ch, hh = h // 2, h % 2
        po = 64 * hh
        for qtp in range(2):
            for kb in range(16):
                sp = 2 + 2 * (kb % 2)
                for i in range(2):
                    qt = 2 * qtp + i
                    em.op("T", lambda e, ch=ch, po=po, kb=kb, sp=sp, i=i, qt=qt:
                          e.matmul(P.bank(sp + i),
                                   k[ch][po:po + 64, 1 + 128 * kb:1 + 128 * kb + 128],
                                   q[ch][po:po + 64, dsl(qt)],
                                   start=True, stop=True, tile_position=(po, 0)),
                          reads=[("ak", (ch, None)), ("aq", (ch, None))],
                          writes=[pk(sp + i)])
                eb = Ebuf[kb % 2]
                em.op("S", lambda e, sp=sp, h=h, kb=kb, eb=eb: e.activation(
                    out=eb[:], in_=P.ps[:, 512 * sp:512 * sp + 1024], func=AF.Exp,
                    bias=ksqT[:, 16 * h + kb:16 * h + kb + 1], scale=0.025),
                    reads=[pk(sp), pk(sp + 1), ("ksqT", None)],
                    writes=[("aE", kb % 2)])
                for i in range(2):
                    em.op("T", lambda e, kb=kb, h=h, i=i, eb=eb: e.matmul(
                        P.bank(0 + i, p=65), v_tm[kb][:, 65 * h:65 * h + 65],
                        eb[:, 512 * i:512 * i + 512],
                        start=(kb == 0), stop=(kb == 15)),
                        reads=[("aE", kb % 2), ("avt", (kb, h)),
                               ("avt", (kb, "ones"))],
                        writes=[pk(0 + i)])
            for i in range(2):
                qt = 2 * qtp + i
                em.op("V", lambda e, i=i: e.reciprocal(
                    rrow[0:1, :], P.bank(0 + i, p0=64, p=1)),
                    reads=[pk(0 + i)], writes=[("arr", None)])
                em.op("T", lambda e, i=i: e.matmul(
                    P.bank(6, p=64), g["ones_row_f"][0:1, 0:64],
                    rrow[0:1, :], start=True, stop=True),
                    reads=[("arr", None), ("ones_row_f", None)], writes=[pk(6)])
                em.op("V", lambda e: e.tensor_copy(rb[:], P.bank(6, p=64)),
                      reads=[pk(6)], writes=[("arb", None)])
                em.op("V", lambda e, ch=ch, hh=hh, qt=qt, i=i: e.tensor_tensor(
                    out=out_a[ch][64 * hh:64 * hh + 64, 512 * qt:512 * qt + 512],
                    in0=P.bank(0 + i, p=64), in1=rb[:], op=ALU.mult),
                    reads=[pk(0 + i), ("arb", None)], writes=[("oa", (ch, (hh, qt)))])
    P.pop()

    if DBG.get("aa_stop") is not None and DBG["aa_stop"] < 4:
        P.pop()
        return
    P.push()
    worow = P.sbr("worow", [128, C], F32)
    wob = P.sbr("wob", [128, C], BF16)
    woT = P.sbr("woT", [128, C], BF16)
    for ot in range(8):
        em.op("SD", lambda e, ot=ot: e.dma_start(
            out=worow[:], in_=H["aWo"][128 * ot:128 * ot + 128, :]),
            writes=[("worow", None)])
        em.op("V", lambda e: e.tensor_copy(wob[:], worow[:]),
              reads=[("worow", None)], writes=[("wob", None)])
        for grp in range(2):
            for c4 in range(4):
                c = 4 * grp + c4
                pb = c4 * 256
                em.op("T", lambda e, c=c, pb=pb: e.transpose(
                    P.psb[:, pb:pb + 128], wob[:, 128 * c:128 * c + 128],
                    g["ident_bf"][:]),
                    reads=[("wob", None), ("ident_bf", None), ("psbg", None)],
                    writes=[("psb", c4), ("psbL", None)])
            for c4 in range(4):
                c = 4 * grp + c4
                pb = c4 * 256
                em.op("V", lambda e, c=c, pb=pb: e.tensor_copy(
                    woT[:, 128 * c:128 * c + 128], P.psb[:, pb:pb + 128]),
                    reads=[("psb", c4), ("psbL", None)],
                    writes=[("woT", c)] + ([("psbg", None)] if c4 == 3 else []))
        for tt in range(4):
            bk = tt % 2

            def chain(e, tt=tt, bk=bk):
                last = None
                for c in range(NCH):
                    last = e.matmul(P.bank(bk), woT[:, 128 * c:128 * c + 128],
                                    out_a[c][:, 512 * tt:512 * tt + 512],
                                    start=(c == 0), stop=(c == NCH - 1))
                return last
            em.op("T", chain,
                  reads=[("woT", c) for c in range(NCH)]
                  + [("oa", (c, None)) for c in range(NCH)],
                  writes=[pk(bk)])
            em.op("V", lambda e, ot=ot, tt=tt, bk=bk: e.scalar_tensor_tensor(
                out=x[ot][:, dsl(tt)], in0=P.bank(bk), scalar=0.0, in1=x[ot][:, dsl(tt)],
                op0=ALU.add, op1=ALU.add),
                reads=[pk(bk), ("x", (ot, None))], writes=[("x", (ot, None))])
    P.pop()
    P.pop()



def xp_view(xp, c, start, w=512):
    off = 1 + start if c < 4 else start
    return xp[c][:, off:off + w]


def _ffn_ln_init(P, x, g, xp, half):
    em = P.em
    hb = 1024 * half
    P.push()
    rows = (P.sbr("flnra", [1, 1024], F32), P.sbr("flnrb", [1, 1024], F32), P.sbr("flnrs", [1, 1024], F32))
    g["f_sqt"] = [P.sbr(f"fsqt{i}", [128, 512], BF16) for i in range(2)]
    xbt = [P.sbr(f"fxbt{i}", [128, 512], BF16) for i in range(2)]
    tmp = [P.sbr(f"ftmp{i}", [128, 512], F32) for i in range(2)]

    def src(c, j):
        i = (2 * c + j) % 2
        em.op("V", lambda e, c=c, j=j, i=i: e.tensor_copy(
            xbt[i][:], x[c][:, 1 + hb + 512 * j:1 + hb + 512 * j + 512]),
            reads=[("x", (c, None))], writes=[("fxbt", i)])
        return [("fxbt", i)], xbt[i][:]
    ln_half(P, g, rows, half, src, "f_")
    for j in range(2):
        ln_bcast(P, g, rows, j, 0, 1)
        for c in range(NCH):
            i = (2 * c + j) % 2
            slp = slice(1 + hb + 512 * j, 1 + hb + 512 * j + 512)
            em.op("V", lambda e, c=c, i=i, slp=slp: e.tensor_tensor(
                out=tmp[i][:], in0=x[c][:, slp], in1=P.bank(0), op=ALU.mult),
                reads=[("x", (c, None)), pk(0)], writes=[("ftmp", i)])
            em.op("V", lambda e, c=c, i=i, slp=slp: e.tensor_tensor(
                out=xp[c][:, slp], in0=tmp[i][:], in1=P.bank(1), op=ALU.add),
                reads=[("ftmp", i), pk(1)], writes=[("xp", (c, None))])
    P.pop()


def _ffn_pair(P, g, xp, tb, v_tm, o_acc, call, l, p):
    em, H = P.em, P.hbm
    P.push()
    wq = P.sbr("fwq", [128, 1024], BF16)
    wk = P.sbr("fwk", [128, 1024], BF16)
    wv = P.sbr("fwv", [128, 1024], BF16)
    qp = P.sbr("fqp", [128, T], BF16)
    kp = P.sbr("fkp", [128, T], BF16)
    sqs = P.sbr("fsqs", [128, 512], BF16)
    qsr = P.sbr("fqsr", [2, 512], F32)
    RSQb = P.sbr("frsq", [128, 512], F32)
    ut = P.sbr("fut", [128, 512], BF16)
    t1 = P.sbr("ft1", [128, 512], BF16)
    t2 = P.sbr("ft2", [128, 512], BF16)
    vft = P.sbr("fvft", [128, 512], BF16)
    Ebuf = [P.sbr(f"fE{i}", [128, 512], BF16) for i in range(2)]
    sadd = P.sbr("fsadd", [128, 512], F32)
    rrow = P.sbr("frr", [1, 512], F32)
    rb = P.sbr("frb", [64, 512], F32)

    for nm, wdst, col0 in (("q", wq, 128 * p), ("k", wk, 512 + 128 * p),
                           ("v", wv, 1024 + 128 * p)):
        em.op("GD", lambda e, wdst=wdst, col0=col0: e.dma_start(
            out=wdst[:], in_=H["fwqkv"][l, :, col0:col0 + 128].rearrange(
                "(cc pp) c -> pp cc c", pp=128)),
            writes=[(f"fw{nm}", None)])

    for nm, wsrc, pt, scol in (("q", wq, qp, 0), ("k", wk, kp, 64)):
        for tt in range(4):
            def chain(e, wsrc=wsrc, tt=tt):
                last = None
                for c in range(NCH):
                    last = e.matmul(P.bank(tt), wsrc[:, 128 * c:128 * c + 128],
                                    xp_view(xp, c, 512 * tt),
                                    start=(c == 0), stop=(c == NCH - 1))
                return last
            em.op("T", chain,
                  reads=[(f"fw{nm}", None)] + [("xp", (c, None)) for c in range(NCH)],
                  writes=[pk(tt)])
        for tt in range(4):
            sl = slice(512 * tt, 512 * tt + 512)
            em.op("S", lambda e, tt=tt: e.activation(
                out=sqs[:], in_=P.bank(tt), func=AF.Square),
                reads=[pk(tt)], writes=[("fsqs", None)])
            em.op("T", lambda e: e.matmul(
                P.bank(4, p=2), g["e2b"][:], sqs[:], start=True, stop=True),
                reads=[("fsqs", None), ("e2b", None)], writes=[pk(4)])
            em.op("S", lambda e: e.activation(
                out=qsr[:], in_=P.bank(4, p=2), func=AF.Sqrt),
                reads=[pk(4)], writes=[("fqsr", None)])
            em.op("V", lambda e: e.tensor_scalar(
                out=qsr[:], in0=qsr[:], scalar1=1e-12, scalar2=None, op0=ALU.max),
                reads=[("fqsr", None)], writes=[("fqsr", None)])
            em.op("V", lambda e: e.reciprocal(qsr[:], qsr[0:2, :]),
                reads=[("fqsr", None)], writes=[("fqsr", None)])
            em.op("T", lambda e: e.matmul(
                P.bank(5), g["e2f"][:], qsr[:], start=True, stop=True),
                reads=[("fqsr", None), ("e2f", None)], writes=[pk(5)])
            sccol = 0 if scol == 0 else 1
            em.op("V", lambda e, sccol=sccol: e.tensor_scalar(
                out=RSQb[:], in0=P.bank(5), scalar1=tb["scv"][:, sccol:sccol + 1],
                scalar2=None, op0=ALU.mult),
                reads=[pk(5), ("fscv", None)], writes=[("frsq", None)])
            em.op("V", lambda e, tt=tt: e.tensor_tensor(
                out=ut[:], in0=P.bank(tt), in1=RSQb[:], op=ALU.mult),
                reads=[pk(tt), ("frsq", None)], writes=[("fut", None)])
            em.op("T", lambda e: e.matmul(
                P.bank(6), g["rotp"][:], ut[:], start=True, stop=True),
                reads=[("fut", None), ("rotp", None)], writes=[pk(6)])
            cst = tb["csq"] if nm == "q" else tb["csk"]
            sst = tb["ssq"] if nm == "q" else tb["ssk"]
            csk_ = "fcsq" if nm == "q" else "fcsk"
            ssk_ = "fssq" if nm == "q" else "fssk"
            em.op("V", lambda e, sl=sl, cst=cst, csk_=csk_: e.tensor_tensor(
                out=t1[:], in0=ut[:], in1=cst[:, sl], op=ALU.mult),
                reads=[("fut", None), (csk_, None)], writes=[("ft1", None)])
            em.op("V", lambda e, sl=sl, sst=sst, ssk_=ssk_: e.tensor_tensor(
                out=t2[:], in0=P.bank(6), in1=sst[:, sl], op=ALU.mult),
                reads=[pk(6), (ssk_, None)], writes=[("ft2", None)])
            em.op("V", lambda e, pt=pt, sl=sl: e.tensor_tensor(
                out=pt[:, sl], in0=t1[:], in1=t2[:], op=ALU.add),
                reads=[("ft1", None), ("ft2", None)],
                writes=[("fqp" if True else "", tt) if pt is qp else ("fkp", tt)])

    for tt in range(4):
        def chainv(e, tt=tt):
            last = None
            for c in range(NCH):
                last = e.matmul(P.bank(tt % 2), wv[:, 128 * c:128 * c + 128],
                                xp_view(xp, c, 512 * tt),
                                start=(c == 0), stop=(c == NCH - 1))
            return last
        em.op("T", chainv,
              reads=[("fwv", None)] + [("xp", (c, None)) for c in range(NCH)],
              writes=[pk(tt % 2)])
        em.op("V", lambda e, tt=tt: e.tensor_copy(vft[:], P.bank(tt % 2)),
              reads=[pk(tt % 2)], writes=[("fvft", None)])
        for i4 in range(4):
            pb = i4 * 256
            em.op("T", lambda e, i4=i4, pb=pb: e.transpose(
                P.psb[:, pb:pb + 128], vft[:, 128 * i4:128 * i4 + 128],
                g["ident_bf"][:]),
                reads=[("fvft", None), ("ident_bf", None), ("psbg", None)],
                writes=[("psb", i4), ("psbL", None)])
        for i4 in range(4):
            kb = 4 * tt + i4
            pb = i4 * 256
            for hf in range(2):
                em.op("V", lambda e, kb=kb, pb=pb, hf=hf: e.tensor_copy(
                    v_tm[kb][:, 65 * hf:65 * hf + 64],
                    P.psb[:, pb + 64 * hf:pb + 64 * hf + 64]),
                    reads=[("psb", i4), ("psbL", None)], writes=[("fvt", (kb, hf))]
                    + ([("psbg", None)] if (i4 == 3 and hf == 1) else []))

    for hf in range(2):
        po = 64 * hf
        for qt in range(4):
            kbs = list(range(max(0, 4 * qt - 4), min(16, 4 * qt + 4)))
            for ki, kb in enumerate(kbs):
                sb = 2 + (ki % 2)
                em.op("T", lambda e, po=po, kb=kb, qt=qt, sb=sb: e.matmul(
                    P.bank(sb), kp[po:po + 64, 128 * kb:128 * kb + 128],
                    qp[po:po + 64, 512 * qt:512 * qt + 512],
                    start=True, stop=True, tile_position=(po, 0)),
                    reads=[("fkp", None), ("fqp", None)], writes=[pk(sb)])
                delta = 512 * qt - 128 * kb
                eb = Ebuf[ki % 2]
                ms = slice(delta + 384, delta + 384 + 512)
                if delta <= 0:
                    em.op("V", lambda e, sb=sb, ms=ms: e.tensor_tensor(
                        out=sadd[:], in0=P.bank(sb), in1=tb["wadd"][:, ms],
                        op=ALU.add),
                        reads=[pk(sb), ("fwadd", None)], writes=[("fsadd", None)])
                    em.op("S", lambda e, eb=eb: e.activation(
                        out=eb[:], in_=sadd[:], func=AF.Exp, scale=8.0),
                        reads=[("fsadd", None)], writes=[("fE", ki % 2)])
                else:
                    em.op("S", lambda e, sb=sb, eb=eb: e.activation(
                        out=eb[:], in_=P.bank(sb), func=AF.Exp, scale=8.0),
                        reads=[pk(sb)], writes=[("fE", ki % 2)])
                    em.op("V", lambda e, eb=eb, ms=ms: e.tensor_tensor(
                        out=eb[:], in0=eb[:], in1=tb["wmul"][:, ms], op=ALU.mult),
                        reads=[("fE", ki % 2), ("fwmul", None)],
                        writes=[("fE", ki % 2)])
                em.op("T", lambda e, kb=kb, hf=hf, eb=eb, st=(ki == 0),
                      sp=(ki == len(kbs) - 1): e.matmul(
                    P.bank(0, p=65), v_tm[kb][:, 65 * hf:65 * hf + 65],
                    eb[:], start=st, stop=sp),
                    reads=[("fE", ki % 2), ("fvt", (kb, hf)), ("fvt", (kb, "o"))],
                    writes=[pk(0)])
            em.op("V", lambda e: e.reciprocal(rrow[0:1, :], P.bank(0, p0=64, p=1)),
                  reads=[pk(0)], writes=[("frr", None)])
            em.op("T", lambda e: e.matmul(
                P.bank(5, p=64), g["ones_row_f"][0:1, 0:64], rrow[0:1, :],
                start=True, stop=True),
                reads=[("frr", None), ("ones_row_f", None)], writes=[pk(5)])
            em.op("V", lambda e: e.tensor_copy(rb[:], P.bank(5, p=64)),
                  reads=[pk(5)], writes=[("frb", None)])
            em.op("V", lambda e, p=p, po=po, qt=qt: e.tensor_tensor(
                out=o_acc[p][po:po + 64, 512 * qt:512 * qt + 512],
                in0=P.bank(0, p=64), in1=rb[:], op=ALU.mult),
                reads=[pk(0), ("frb", None)], writes=[("oac", (p, (po, qt)))])
    P.pop()


def _ffn_wo(P, g, xp, o_acc, call, l):
    em, H = P.em, P.hbm
    P.push()
    wot = [P.sbr(f"fwot{i}", [128, 512], BF16) for i in range(2)]
    for ot in range(8):
        r = ot % 2
        em.op("GD", lambda e, ot=ot, r=r: e.dma_start(
            out=wot[r][:], in_=H["fwo"][l, :, 128 * ot:128 * ot + 128].rearrange(
                "(cc pp) c -> pp cc c", pp=128)),
            writes=[("fwot", r)])
        for tt in range(4):
            bk = tt % 2
            def chain(e, r=r, tt=tt, bk=bk):
                last = None
                for c in range(4):
                    last = e.matmul(P.bank(bk), wot[r][:, 128 * c:128 * c + 128],
                                    o_acc[c][:, 512 * tt:512 * tt + 512],
                                    start=(c == 0), stop=(c == 3))
                return last
            em.op("T", chain,
                  reads=[("fwot", r)] + [("oac", (c, None)) for c in range(4)],
                  writes=[pk(bk)])
            em.op("V", lambda e, ot=ot, tt=tt, bk=bk: e.scalar_tensor_tensor(
                out=xp[ot][:, dsl(tt)], in0=P.bank(bk), scalar=0.0,
                in1=xp[ot][:, dsl(tt)], op0=ALU.add, op1=ALU.add),
                reads=[pk(bk), ("xp", (ot, None))], writes=[("xp", (ot, None))])
    P.pop()


def _ffn_layer_attn(P, x, g, xp, call, l):
    em, H = P.em, P.hbm
    P.push()
    tb = {}
    tb["csq"] = P.sbr("fcsq", [128, T], BF16)
    tb["csk"] = P.sbr("fcsk", [128, T], BF16)
    tb["ssq"] = P.sbr("fssq", [128, T], BF16)
    tb["ssk"] = P.sbr("fssk", [128, T], BF16)
    tb["wadd"] = P.sbr("fwaddt", [128, 1408], BF16)
    tb["wmul"] = P.sbr("fwmult", [128, 1408], BF16)
    tb["scv"] = P.sbr("fscv", [128, 2], F32)
    o_acc = [P.sbr(f"foac{c}", [128, T], BF16) for c in range(4)]
    v_tm = [P.sbr(f"fvt{t}", [128, 130], BF16) for t in range(16)]
    for nm, key in (("rot_csq", "fcsq"), ("rot_csk", "fcsk"),
                    ("rot_ssq", "fssq"), ("rot_ssk", "fssk")):
        dstt = tb[key[1:]]
        em.op("GD", lambda e, nm=nm, dstt=dstt: e.dma_start(out=dstt[:], in_=H[nm]),
              writes=[(key, None)])
    for nm, key in (("win_add", "fwadd"), ("win_mul", "fwmul")):
        dstt = tb[key[1:]]
        em.op("SD", lambda e, nm=nm, dstt=dstt: e.dma_start(out=dstt[:], in_=H[nm]),
              writes=[(key, None)])
    for hf in range(2):
        em.op("SD", lambda e, hf=hf: e.dma_start(
            out=tb["scv"][64 * hf:64 * hf + 64, 0:1],
            in_=H["fqs"][l, :].rearrange("(k p) -> p k", p=64),
            allow_slow_non_contiguous=True),
            writes=[("fscv", None)])
        em.op("SD", lambda e, hf=hf: e.dma_start(
            out=tb["scv"][64 * hf:64 * hf + 64, 1:2],
            in_=H["fks"][l, :].rearrange("(k p) -> p k", p=64),
            allow_slow_non_contiguous=True),
            writes=[("fscv", None)])
    for kb in range(16):
        for hf in range(2):
            em.op("V", lambda e, kb=kb, hf=hf: e.memset(
                v_tm[kb][:, 65 * hf + 64:65 * hf + 65], 1.0),
                writes=[("fvt", (kb, "o"))])
    for p in range(4):
        _ffn_pair(P, g, xp, tb, v_tm, o_acc, call, l, p)
    _ffn_wo(P, g, xp, o_acc, call, l)
    P.pop()


def _ffn_layer_mlp(P, g, xp, stash, call, l, th):
    em, H = P.em, P.hbm
    hb = 1024 * th
    P.push()
    rows = (P.sbr("fmra", [1, 1024], F32), P.sbr("fmrb", [1, 1024], F32), P.sbr("fmrs", [1, 1024], F32))
    g["m_sqt"] = [P.sbr(f"msqt{i}", [128, 512], BF16) for i in range(2)]
    tmp = [P.sbr(f"mtmp{i}", [128, 512], F32) for i in range(2)]
    gcol = P.sbr("mgc", [128, 8], F32)
    lbc = P.sbr("mlbc", [128, 8], BF16)
    b1a = P.sbr("mb1a", [128, 22], F32)
    b1g = P.sbr("mb1g", [128, 22], F32)
    b2c = P.sbr("mb2c", [128, 8], F32)
    b1p = P.sbr("mb1p", [128, 1], F32)
    xh = [P.sbr(f"mxh{c}", [128, 1024], BF16) for c in range(NCH)]
    DBG["names"][f"mxh{call}{l}{th}"] = [t.name for t in xh]
    h2 = [P.sbr(f"mh2{c}", [128, 1024], BF16) for c in range(22)]
    abuf = P.sbr("mab", [128, 1024], BF16)
    gbuf = P.sbr("mgb", [128, 1024], BF16)
    w1t = [P.sbr(f"mw1{i}", [128, 1024], BF16) for i in range(2)]
    w2t = P.sbr("mw2", [128, 2816], BF16)

    em.op("SD", lambda e: e.dma_start(
        out=gcol[:], in_=H["flng"][l, :].rearrange("(k p) -> p k", p=128),
        allow_slow_non_contiguous=True), writes=[("mgc", None)])
    em.op("GD", lambda e: e.dma_start(
        out=lbc[:], in_=H["flnb"][l, :].rearrange("(k p) -> p k", p=128),
        allow_slow_non_contiguous=True), writes=[("mlbc", None)])
    em.op("SD", lambda e: e.dma_start(
        out=b1a[:, 0:21], in_=H["fb1"][l, 0:2688].rearrange("(k p) -> p k", p=128),
        allow_slow_non_contiguous=True), writes=[("mb1a", None)])
    em.op("SD", lambda e: e.dma_start(
        out=b1a[0:42, 21:22], in_=H["fb1"][l, 2688:2730].rearrange(
            "(k p) -> p k", p=42), allow_slow_non_contiguous=True),
        writes=[("mb1a", None)])
    em.op("SD", lambda e: e.dma_start(
        out=b1g[:, 0:21], in_=H["fb1"][l, 2730:5418].rearrange(
            "(k p) -> p k", p=128), allow_slow_non_contiguous=True),
        writes=[("mb1g", None)])
    em.op("SD", lambda e: e.dma_start(
        out=b1g[0:42, 21:22], in_=H["fb1"][l, 5418:5460].rearrange(
            "(k p) -> p k", p=42), allow_slow_non_contiguous=True),
        writes=[("mb1g", None)])
    em.op("SD", lambda e: e.dma_start(
        out=b2c[:], in_=H["fb2"][l, :].rearrange("(k p) -> p k", p=128),
        allow_slow_non_contiguous=True), writes=[("mb2c", None)])

    spl = None
    if th == 1:
        spl = [P.sbr(f"mspl{c4}", [128, 512], BF16) for c4 in range(4)]
        for c4 in range(4):
            em.op("V", lambda e, c4=c4: e.tensor_copy(
                spl[c4][:], xp[4 + c4][:, 1024:1536]),
                reads=[("xp", (4 + c4, None))], writes=[("mspl", c4)])
            em.op("V", lambda e, c4=c4: e.tensor_copy(
                spl[c4][:, 0:1], stash[:, c4:c4 + 1]),
                reads=[("fstash", c4)], writes=[("mspl", c4)])

    def src(c, j):
        if th == 1 and c >= 4 and j == 0:
            return [("mspl", c - 4)], spl[c - 4][:]
        return [("xp", (c, None))], xp_view(xp, c, hb + 512 * j)
    ln_half(P, g, rows, th, src, "m_")
    for j in range(2):
        ln_bcast(P, g, rows, j, 0, 1)
        for c in range(NCH):
            i = (2 * c + j) % 2
            rk2, inap = ((["PLACE"], None))
            if th == 1 and c >= 4 and j == 0:
                rk2, inap = [("mspl", c - 4)], spl[c - 4][:]
            else:
                rk2, inap = [("xp", (c, None))], xp_view(xp, c, hb + 512 * j)
            em.op("V", lambda e, i=i, inap=inap: e.tensor_tensor(
                out=tmp[i][:], in0=inap, in1=P.bank(0), op=ALU.mult),
                reads=rk2 + [pk(0)], writes=[("mtmp", i)])
            em.op("V", lambda e, c=c, j=j, i=i: e.tensor_tensor(
                out=tmp[i][:], in0=tmp[i][:], in1=P.bank(1), op=ALU.add),
                reads=[("mtmp", i), pk(1)], writes=[("mtmp", i)])
            em.op("V", lambda e, c=c, j=j, i=i: e.tensor_scalar(
                out=xh[c][:, 512 * j:512 * j + 512], in0=tmp[i][:],
                scalar1=gcol[:, c:c + 1], scalar2=None, op0=ALU.mult),
                reads=[("mtmp", i), ("mgc", None)], writes=[("mxh", c)])
    for c22 in range(22):
        rn = 128 if c22 < 21 else 42
        for side in ("a", "g"):
            r = 0 if side == "a" else 1
            col0 = 128 * c22 + (0 if side == "a" else 2730)
            em.op("GD", lambda e, col0=col0, rn=rn, r=r: e.dma_start(
                out=w1t[r][:, 0:8 * rn],
                in_=H["fw1"][l, :, col0:col0 + rn].rearrange(
                    "(cc pp) c -> pp cc c", pp=128)),
                writes=[("mw1", r)])
            def blb(e, rn=rn, r=r):
                last = None
                for c in range(NCH):
                    last = e.matmul(P.bank(4, w=1, p=rn),
                                    w1t[r][:, rn * c:rn * c + rn],
                                    lbc[:, c:c + 1], start=(c == 0),
                                    stop=(c == NCH - 1))
                return last
            em.op("T", blb, reads=[("mw1", r), ("mlbc", None)], writes=[pk(4)])
            bsel = b1a if side == "a" else b1g
            em.op("V", lambda e, c22=c22, rn=rn, bsel=bsel: e.tensor_scalar(
                out=b1p[0:rn, :], in0=P.bank(4, w=1, p=rn),
                scalar1=bsel[0:rn, c22:c22 + 1] if False else None,
                scalar2=None, op0=ALU.add) if False else
                e.tensor_tensor(out=b1p[0:rn, :], in0=P.bank(4, w=1, p=rn),
                                in1=bsel[0:rn, c22:c22 + 1], op=ALU.add),
                reads=[pk(4), ("mb1a" if bsel is b1a else "mb1g", None)],
                writes=[("mb1p", None)])
            for j in range(2):
                bk = j
                def chain(e, rn=rn, r=r, j=j, bk=bk):
                    last = None
                    for c in range(NCH):
                        last = e.matmul(P.bank(bk, p=rn),
                                        w1t[r][:, rn * c:rn * c + rn],
                                        xh[c][:, 512 * j:512 * j + 512],
                                        start=(c == 0), stop=(c == NCH - 1))
                    return last
                em.op("T", chain,
                      reads=[("mw1", r), ("mb1p", None)]
                      + [("mxh", c) for c in range(NCH)],
                      writes=[pk(bk)])
                dstb = abuf if side == "a" else gbuf
                fn = AF.Identity if side == "a" else AF.Gelu
                em.op("S", lambda e, rn=rn, j=j, bk=bk, dstb=dstb, fn=fn:
                      e.activation(out=dstb[0:rn, 512 * j:512 * j + 512],
                                   in_=P.bank(bk, p=rn), func=fn,
                                   bias=b1p[0:rn, :], scale=1.0),
                      reads=[pk(bk), ("mb1p", None)],
                      writes=[("mab" if side == "a" else "mgb", j)])
        em.op("V", lambda e, c22=c22, rn=rn: e.tensor_tensor(
            out=h2[c22][0:rn, :], in0=abuf[0:rn, :], in1=gbuf[0:rn, :],
            op=ALU.mult),
            reads=[("mab", None), ("mgb", None)], writes=[("mh2", c22)])
    for ot in range(8):
        em.op("GD", lambda e, ot=ot: e.dma_start(
            out=w2t[:, 0:2688], in_=H["fw2"][l, 0:2688, 128 * ot:128 * ot + 128]
            .rearrange("(cc pp) c -> pp cc c", pp=128)),
            writes=[("mw2", 0)])
        em.op("GD", lambda e, ot=ot: e.dma_start(
            out=w2t[0:42, 2688:2816], in_=H["fw2"][l, 2688:2730,
                                                   128 * ot:128 * ot + 128]),
            writes=[("mw2", 1)])
        for j in range(2):
            bk = j
            def chain(e, j=j, bk=bk):
                last = None
                for c in range(22):
                    rn = 128 if c < 21 else 42
                    last = e.matmul(P.bank(bk), w2t[0:rn, 128 * c:128 * c + 128],
                                    h2[c][0:rn, 512 * j:512 * j + 512],
                                    start=(c == 0), stop=(c == 21))
                return last
            em.op("T", chain,
                  reads=[("mw2", None)] + [("mh2", c) for c in range(22)],
                  writes=[pk(bk)])
            em.op("V", lambda e, ot=ot, j=j, bk=bk: e.scalar_tensor_tensor(
                out=xp[ot][:, 1 + hb + 512 * j:1 + hb + 512 * j + 512],
                in0=P.bank(bk), scalar=b2c[:, ot:ot + 1],
                in1=xp[ot][:, 1 + hb + 512 * j:1 + hb + 512 * j + 512],
                op0=ALU.add, op1=ALU.add),
                reads=[pk(bk), ("mb2c", None), ("xp", (ot, None))],
                writes=[("xp", (ot, None))])
    P.pop()


def ffn_call(P, x, g, call):
    em = P.em
    P.push()
    xp = [P.sbr(f"xp{c}", [128, TP], BF16) for c in range(NCH)]
    DBG["names"][f"xp{call}"] = [t.name for t in xp]
    for c in range(NCH):
        em.op("V", lambda e, c=c: e.memset(xp[c][:, 0:1], 0.0),
              writes=[("xp", (c, "p"))])
    for half in range(2):
        _ffn_ln_init(P, x, g, xp, half)
    stash = P.sbr("fstash", [128, 4], BF16)
    step = 1
    stop = DBG.get("ffn_stop")
    for l in range(2):
        if stop is None or step < stop:
            _ffn_layer_attn(P, x, g, xp, call, l)
        step += 1
        for c4 in range(4):
            em.op("V", lambda e, c4=c4: e.tensor_copy(
                stash[:, c4:c4 + 1], xp[4 + c4][:, 1024:1025]),
                reads=[("xp", (4 + c4, None))], writes=[("fstash", c4)])
        for th in range(2):
            if stop is None or step < stop:
                _ffn_layer_mlp(P, g, xp, stash, call, l, th)
            step += 1
    for c in range(NCH):
        for tt in range(4):
            em.op("V", lambda e, c=c, tt=tt: e.tensor_tensor(
                out=x[c][:, dsl(tt)], in0=x[c][:, dsl(tt)], in1=xp[c][:, dsl(tt)],
                op=ALU.add),
                reads=[("x", (c, None)), ("xp", (c, None))],
                writes=[("x", (c, None))])
    if not DBG["keep"]:
        P.pop()



def _attn_b_half(P, x, g, q, k, v_tm, half):
    em, H = P.em, P.hbm
    hb = 1024 * half
    P.push()
    xh = [P.sbr(f"bxh{c}", [128, 1024], BF16) for c in range(NCH)]
    P.push()
    rows = (P.sbr("blnra", [1, 1024], F32), P.sbr("blnrb", [1, 1024], F32), P.sbr("blnrs", [1, 1024], F32))
    g["b_sqt"] = [P.sbr(f"bsqt{i}", [128, 512], BF16) for i in range(2)]
    xbt = [P.sbr(f"bxbt{i}", [128, 512], BF16) for i in range(2)]
    tmp = [P.sbr(f"btmp{i}", [128, 512], F32) for i in range(2)]

    def src(c, j):
        i = (2 * c + j) % 2
        em.op("V", lambda e, c=c, j=j, i=i: e.tensor_copy(
            xbt[i][:], x[c][:, 1 + hb + 512 * j:1 + hb + 512 * j + 512]),
            reads=[("x", (c, None))], writes=[("bxbt", i)])
        return [("bxbt", i)], xbt[i][:]
    ln_half(P, g, rows, half, src, "b_")
    for j in range(2):
        ln_bcast(P, g, rows, j, 0, 1)
        for c in range(NCH):
            i = (2 * c + j) % 2
            slp = slice(1 + hb + 512 * j, 1 + hb + 512 * j + 512)
            em.op("V", lambda e, c=c, i=i, slp=slp: e.tensor_tensor(
                out=tmp[i][:], in0=x[c][:, slp], in1=P.bank(0), op=ALU.mult),
                reads=[("x", (c, None)), pk(0)], writes=[("btmp", i)])
            em.op("V", lambda e, c=c, i=i, j=j: e.tensor_tensor(
                out=xh[c][:, 512 * j:512 * j + 512], in0=tmp[i][:], in1=P.bank(1),
                op=ALU.add),
                reads=[("btmp", i), pk(1)], writes=[("bxh", c)])
    P.pop()

    wcol = [P.sbr(f"bwc{i}", [128, 1024], BF16) for i in range(2)]
    vft = P.sbr("bvft", [128, 512], BF16)
    for nm, wnm, dst in (("q", "bWq", q), ("k", "bWk", k), ("v", "bWv", None)):
        for ot in range(8):
            r = ot % 2
            em.op("GD", lambda e, wnm=wnm, ot=ot, r=r: e.dma_start(
                out=wcol[r][:], in_=H[wnm][:, 128 * ot:128 * ot + 128].rearrange(
                    "(cc pp) c -> pp cc c", pp=128)),
                writes=[("bwc", r)])
            for j in range(2):
                bk = j
                def chain(e, r=r, j=j, bk=bk):
                    last = None
                    for c in range(NCH):
                        last = e.matmul(P.bank(bk), wcol[r][:, 128 * c:128 * c + 128],
                                        xh[c][:, 512 * j:512 * j + 512],
                                        start=(c == 0), stop=(c == NCH - 1))
                    return last
                em.op("T", chain,
                      reads=[("bwc", r)] + [("bxh", c) for c in range(NCH)],
                      writes=[pk(bk)])
                if dst is not None:
                    nmk = "bq" if dst is q else "bk"
                    em.op("V", lambda e, dst=dst, ot=ot, j=j, bk=bk: e.tensor_copy(
                        dst[ot][:, hb + 512 * j:hb + 512 * j + 512], P.bank(bk)),
                        reads=[pk(bk)], writes=[(nmk, (ot, None))])
                else:
                    em.op("V", lambda e, bk=bk: e.tensor_copy(vft[:], P.bank(bk)),
                          reads=[pk(bk)], writes=[("bvft", None)])
                    for i4 in range(4):
                        pb = i4 * 256
                        em.op("T", lambda e, i4=i4, pb=pb: e.transpose(
                            P.psb[:, pb:pb + 128], vft[:, 128 * i4:128 * i4 + 128],
                            g["ident_bf"][:]),
                            reads=[("bvft", None), ("ident_bf", None),
                                   ("psbg", None)],
                            writes=[("psb", i4), ("psbL", None)])
                    for i4 in range(4):
                        kb = 8 * half + 4 * j + i4
                        pb = i4 * 256
                        em.op("V", lambda e, kb=kb, ot=ot, pb=pb: e.tensor_copy(
                            v_tm[kb][:, 65 * 2 * ot:65 * 2 * ot + 64],
                            P.psb[:, pb:pb + 64]),
                            reads=[("psb", i4), ("psbL", None)],
                            writes=[("bvt", (kb, 2 * ot))])
                        em.op("V", lambda e, kb=kb, ot=ot, pb=pb: e.tensor_copy(
                            v_tm[kb][:, 65 * (2 * ot + 1):65 * (2 * ot + 1) + 64],
                            P.psb[:, pb + 64:pb + 128]),
                            reads=[("psb", i4), ("psbL", None)],
                            writes=[("bvt", (kb, 2 * ot + 1))]
                            + ([("psbg", None)] if i4 == 3 else []))
    P.pop()


def _attn_b_head(P, g, q, k, v_tm, out_b, bufs, h):
    em = P.em
    ch, hh = h // 2, h % 2
    Ebuf, frow, Fb, att, tbuf, sqa, rrow, rb, lamt, w64 = bufs
    for qt in range(4):
        kbs = list(range(0, 4 * qt + 4))
        for ki, kb in enumerate(kbs):
            sp = 2 + 2 * (ki % 2)
            for jj in range(2):
                po = 64 * hh + 32 * jj
                em.op("T", lambda e, ch=ch, po=po, kb=kb, qt=qt, sp=sp, jj=jj:
                      e.matmul(P.bank(sp + jj),
                               k[ch][po:po + 32, 128 * kb:128 * kb + 128],
                               q[ch][po:po + 32, 512 * qt:512 * qt + 512],
                               start=True, stop=True, tile_position=(po, 0)),
                      reads=[("bk", (ch, None)), ("bq", (ch, None))],
                      writes=[pk(sp + jj)])
            eb = Ebuf[ki % 2]
            em.op("S", lambda e, sp=sp, eb=eb: e.activation(
                out=eb[:], in_=P.ps[:, 512 * sp:512 * sp + 1024], func=AF.Exp,
                scale=ATTB_SCALE),
                reads=[pk(sp), pk(sp + 1)], writes=[("bE", None)])
            if kb >= 4 * qt:
                d = 128 * kb - 512 * qt
                em.op("V", lambda e, eb=eb, d=d: e.tensor_tensor(
                    out=eb[:].rearrange("p (a b) -> p a b", a=2),
                    in0=eb[:].rearrange("p (a b) -> p a b", a=2),
                    in1=g["tri"][:, 384 - d:384 - d + 512]
                    .rearrange("p (a b) -> p a b", a=1)
                    .broadcast_to([128, 2, 512]), op=ALU.mult),
                    reads=[("bE", None), ("tri", None)], writes=[("bE", None)])
            for jj in range(2):
                em.op("T", lambda e, kb=kb, h=h, jj=jj, eb=eb, st=(ki == 0),
                      sp2=(ki == len(kbs) - 1): e.matmul(
                    P.bank(jj, p=65), v_tm[kb][:, 65 * h:65 * h + 65],
                    eb[:, 512 * jj:512 * jj + 512], start=st, stop=sp2),
                    reads=[("bE", None), ("bvt", (kb, h)), ("bvt", (kb, "o"))],
                    writes=[pk(jj)])
        em.op("V", lambda e: e.reciprocal(frow[0:1, :], P.bank(0, p0=64, p=1)),
              reads=[pk(0)], writes=[("bfr", 0)])
        em.op("V", lambda e: e.reciprocal(frow[32:33, :], P.bank(1, p0=64, p=1)),
              reads=[pk(1)], writes=[("bfr", 1)])
        em.op("V", lambda e: e.tensor_scalar(
            out=frow[32:33, :], in0=frow[32:33, :], scalar1=lamt[32:33, :],
            scalar2=None, op0=ALU.mult),
            reads=[("bfr", 1), ("blam", None)], writes=[("bfr", 1)])
        em.op("T", lambda e: e.matmul(
            P.bank(4, p=64), g["ones_row_f"][0:1, 0:64], frow[0:1, :],
            start=True, stop=True),
            reads=[("bfr", 0), ("ones_row_f", None)], writes=[pk(4)])
        em.op("V", lambda e: e.tensor_copy(Fb[:], P.bank(4, p=64)),
              reads=[pk(4)], writes=[("bFb", None)])
        em.op("V", lambda e, qt=qt: e.tensor_tensor(
            out=att[:], in0=P.bank(0, p=64),
            in1=Fb[:], op=ALU.mult),
            reads=[pk(0), ("bFb", None)], writes=[("batt", None)])
        em.op("T", lambda e: e.matmul(
            P.bank(4, p=64), g["ones_row_f"][32:33, 0:64], frow[32:33, :],
            start=True, stop=True, tile_position=(32, 0)),
            reads=[("bfr", 1), ("ones_row_f", None), ("bFb", None)], writes=[pk(4)])
        em.op("V", lambda e: e.tensor_copy(Fb[:], P.bank(4, p=64)),
              reads=[pk(4)], writes=[("bFb", None)])
        em.op("V", lambda e: e.tensor_tensor(
            out=tbuf[:], in0=P.bank(1, p=64), in1=Fb[:], op=ALU.mult),
            reads=[pk(1), ("bFb", None)], writes=[("btb", None)])
        em.op("V", lambda e, qt=qt: e.tensor_tensor(
            out=att[:], in0=att[:], in1=tbuf[:], op=ALU.subtract),
            reads=[("batt", None), ("btb", None)], writes=[("batt", None)])
        em.op("S", lambda e: e.activation(out=sqa[:], in_=att[:], func=AF.Square),
              reads=[("batt", None)], writes=[("bsqa", None)])
        em.op("T", lambda e: e.matmul(
            P.bank(5, p0=0, p=1), g["ones_col_b"][0:64, :], sqa[:],
            start=True, stop=True),
            reads=[("bsqa", None), ("ones_col_b", None)], writes=[pk(5)])
        em.op("S", lambda e: e.activation(
            out=rrow[0:1, :], in_=P.bank(5, p0=0, p=1), func=AF.Sqrt,
            bias=g["epsr"][0:1, :], scale=1.0 / 64.0),
            reads=[pk(5), ("epsr", None), ("bfr", None)], writes=[("bfr", None)])
        em.op("V", lambda e: e.reciprocal(rrow[0:1, :], rrow[0:1, :]),
              reads=[("bfr", None)], writes=[("bfr", None)])
        em.op("T", lambda e: e.matmul(
            P.bank(6, p=64), g["ones_row_f"][0:1, 0:64], rrow[0:1, :],
            start=True, stop=True),
            reads=[("bfr", None), ("ones_row_f", None)], writes=[pk(6)])
        em.op("V", lambda e, ch=ch, hh=hh, qt=qt: e.scalar_tensor_tensor(
            out=out_b[ch][64 * hh:64 * hh + 64, 512 * qt:512 * qt + 512],
            in0=att[:], scalar=w64[:, 0:1],
            in1=P.bank(6, p=64), op0=ALU.mult, op1=ALU.mult),
            reads=[("batt", None), pk(6), ("bw64", None)],
            writes=[("outb", (ch, (hh, qt)))])


def attn_b(P, x, g):
    em, H = P.em, P.hbm
    P.push()
    q = [P.sbr(f"bq{c}", [128, T], BF16) for c in range(NCH)]
    k = [P.sbr(f"bk{c}", [128, T], BF16) for c in range(NCH)]
    v_tm = [P.sbr(f"bvt{t}", [128, 16 * 65], BF16) for t in range(16)]
    for kb in range(16):
        em.op("V", lambda e, kb=kb: e.memset(
            v_tm[kb][:].rearrange("p (h c) -> p h c", c=65)[:, :, 64:65], 1.0),
            writes=[("bvt", (kb, "o"))])
    for half in range(2):
        _attn_b_half(P, x, g, q, k, v_tm, half)

    out_b = [P.sbr(f"outb{c}", [128, T], BF16) for c in range(NCH)]
    P.push()
    lamt = P.sbr("blamt", [33, 1], F32)
    w64 = P.sbr("bw64", [64, 1], F32)
    P.push()
    lrow = P.sbr("blrow", [1, 128], F32)
    lscr = P.sbr("blscr", [1, 4], F32)
    for i, nm in enumerate(("lq1", "lk1", "lq2", "lk2")):
        em.op("SD", lambda e, i=i, nm=nm: e.dma_start(
            out=lrow[0:1, 32 * i:32 * i + 32], in_=H[nm]),
            writes=[("blrow", i)])
    em.op("V", lambda e: e.tensor_tensor(
        out=lscr[0:1, 0:1] if False else lrow[0:1, 0:32],
        in0=lrow[0:1, 0:32], in1=lrow[0:1, 32:64], op=ALU.mult),
        reads=[("blrow", None)], writes=[("blrow", "p1")])
    em.op("V", lambda e: e.tensor_reduce(
        out=lscr[0:1, 0:1], in_=lrow[0:1, 0:32], axis=AXX, op=ALU.add),
        reads=[("blrow", "p1")], writes=[("blscr", 0)])
    em.op("V", lambda e: e.tensor_tensor(
        out=lrow[0:1, 64:96], in0=lrow[0:1, 64:96], in1=lrow[0:1, 96:128],
        op=ALU.mult),
        reads=[("blrow", None)], writes=[("blrow", "p2")])
    em.op("V", lambda e: e.tensor_reduce(
        out=lscr[0:1, 1:2], in_=lrow[0:1, 64:96], axis=AXX, op=ALU.add),
        reads=[("blrow", "p2")], writes=[("blscr", 1)])
    em.op("S", lambda e: e.activation(
        out=lscr[0:1, 2:4], in_=lscr[0:1, 0:2], func=AF.Exp),
        reads=[("blscr", None)], writes=[("blscr", 2)])
    em.op("V", lambda e: e.tensor_tensor(
        out=lamt[0:1, :], in0=lscr[0:1, 2:3], in1=lscr[0:1, 3:4], op=ALU.subtract),
        reads=[("blscr", 2)], writes=[("blam", None)])
    em.op("V", lambda e: e.tensor_scalar(
        out=lamt[0:1, :], in0=lamt[0:1, :], scalar1=float(LAM_INIT), scalar2=None,
        op0=ALU.add),
        reads=[("blam", None)], writes=[("blam", None)])
    em.op("V", lambda e: e.tensor_copy(lamt[32:33, :], lamt[0:1, :]),
          reads=[("blam", None)], writes=[("blam", None)])
    P.pop()
    em.op("SD", lambda e: e.dma_start(
        out=w64[:, 0:1], in_=H["rms_w"].rearrange("(k p) -> p k", p=64),
        allow_slow_non_contiguous=True),
        writes=[("bw64", None)])
    em.op("V", lambda e: e.tensor_scalar(
        out=w64[:], in0=w64[:], scalar1=float(1.0 - LAM_INIT), scalar2=None,
        op0=ALU.mult),
        reads=[("bw64", None)], writes=[("bw64", None)])

    _bE = P.sbr("bE0", [128, 1024], BF16)
    Ebuf = [_bE, _bE]
    g["tri"] = P.sbr("btri", [128, 896], BF16)
    em.op("SD", lambda e: e.dma_start(out=g["tri"][:], in_=H["tri"]),
          writes=[("tri", None)])
    frow = P.sbr("bfrow", [33, 512], F32)
    Fb = P.sbr("bFbb", [64, 512], F32)
    att = P.sbr("batt", [64, 512], F32)
    tbuf = P.sbr("btbuf", [64, 512], F32)
    sqa = P.sbr("bsqa", [64, 512], BF16)
    rrow = frow
    rb = None
    bufs = (Ebuf, frow, Fb, att, tbuf, sqa, rrow, rb, lamt, w64)
    for h in range(16):
        _attn_b_head(P, g, q, k, v_tm, out_b, bufs, h)
    P.pop()

    P.push()
    wcol = [P.sbr(f"bwoc{i}", [128, 1024], BF16) for i in range(2)]
    for ot in range(8):
        r = ot % 2
        em.op("GD", lambda e, ot=ot, r=r: e.dma_start(
            out=wcol[r][:], in_=H["bWo"][:, 128 * ot:128 * ot + 128].rearrange(
                "(cc pp) c -> pp cc c", pp=128)),
            writes=[("bwoc", r)])
        for tt in range(4):
            bk = tt % 2
            def chain(e, r=r, tt=tt, bk=bk):
                last = None
                for c in range(NCH):
                    last = e.matmul(P.bank(bk), wcol[r][:, 128 * c:128 * c + 128],
                                    out_b[c][:, 512 * tt:512 * tt + 512],
                                    start=(c == 0), stop=(c == NCH - 1))
                return last
            em.op("T", chain,
                  reads=[("bwoc", r)] + [("outb", (c, None)) for c in range(NCH)],
                  writes=[pk(bk)])
            em.op("V", lambda e, ot=ot, tt=tt, bk=bk: e.scalar_tensor_tensor(
                out=x[ot][:, dsl(tt)], in0=P.bank(bk), scalar=0.0,
                in1=x[ot][:, dsl(tt)], op0=ALU.add, op1=ALU.add),
                reads=[pk(bk), ("x", (ot, None))], writes=[("x", (ot, None))])
    P.pop()
    P.pop()



def build_nc(upto=4):
    nc = bass.Bass("TRN2", target_bir_lowering=False, debug=False)
    H = {}
    H["x_in"] = nc.dram_tensor("x_in", [T, C], F32, kind="ExternalInput").ap()
    for nm in ("aWq", "aWk", "aWv", "aWo", "bWq", "bWk", "bWv", "bWo"):
        H[nm] = nc.dram_tensor(nm, [C, C], F32, kind="ExternalInput").ap()
    for nm in ("agq", "agk", "agv"):
        H[nm] = nc.dram_tensor(nm, [C], F32, kind="ExternalInput").ap()
    for nm in ("lq1", "lk1", "lq2", "lk2"):
        H[nm] = nc.dram_tensor(nm, [1, 32], F32, kind="ExternalInput").ap()
    H["rms_w"] = nc.dram_tensor("rms_w", [64], F32, kind="ExternalInput").ap()
    H["fwqkv"] = nc.dram_tensor("fwqkv", [2, C, 1536], F32, kind="ExternalInput").ap()
    H["fqs"] = nc.dram_tensor("fqs", [2, 64], F32, kind="ExternalInput").ap()
    H["fks"] = nc.dram_tensor("fks", [2, 64], F32, kind="ExternalInput").ap()
    H["fwo"] = nc.dram_tensor("fwo", [2, 512, C], F32, kind="ExternalInput").ap()
    H["flng"] = nc.dram_tensor("flng", [2, C], F32, kind="ExternalInput").ap()
    H["flnb"] = nc.dram_tensor("flnb", [2, C], F32, kind="ExternalInput").ap()
    H["fw1"] = nc.dram_tensor("fw1", [2, C, 5460], F32, kind="ExternalInput").ap()
    H["fb1"] = nc.dram_tensor("fb1", [2, 5460], F32, kind="ExternalInput").ap()
    H["fw2"] = nc.dram_tensor("fw2", [2, HID, C], F32, kind="ExternalInput").ap()
    H["fb2"] = nc.dram_tensor("fb2", [2, C], F32, kind="ExternalInput").ap()
    for nm, shp in (("ident_bf", [128, 128]), ("win_add", [128, 1408]),
                    ("win_mul", [128, 1408]), ("tri", [128, 896]),
                    ("rotp", [128, 128])):
        H[nm] = nc.dram_tensor(nm, shp, BF16, kind="ExternalInput").ap()
    for nm in ("rot_csq", "rot_csk", "rot_ssq", "rot_ssk"):
        H[nm] = nc.dram_tensor(nm, [128, T], F32, kind="ExternalInput").ap()
    H["ident_f32"] = nc.dram_tensor("ident_f32", [128, 128], F32,
                                    kind="ExternalInput").ap()
    H["e2f"] = nc.dram_tensor("e2f", [2, 128], F32, kind="ExternalInput").ap()
    H["e2b"] = nc.dram_tensor("e2b", [128, 2], BF16, kind="ExternalInput").ap()
    H["out"] = nc.dram_tensor("out", [T, C], F32, kind="ExternalOutput").ap()

    with ExitStack() as ctx:
        em = Em(nc, ctx)
        P = Prog(nc, ctx, em)
        P.hbm = H
        x = [P.sb(f"x{c}", [128, TP], F32) for c in range(NCH)]
        g = {}
        g["ident_bf"] = P.sb("identb", [128, 128], BF16)
        g["rotp"] = P.sb("rotp_sb", [128, 128], BF16)
        g["ones_row_f"] = P.sb("onesrf", [65, 128], F32)
        g["ones_row_b"] = P.sb("onesrb", [65, 128], BF16)
        g["ones_col_b"] = P.sb("onescb", [128, 1], BF16)
        g["e2f"] = P.sb("e2f", [2, 128], F32)
        g["e2b"] = P.sb("e2b", [128, 2], BF16)
        g["epsr"] = P.sb("epsr", [65, 1], F32)
        P.ps = ctx.enter_context(nc.psum_tensor("psA", [128, 7 * 512], F32))
        P.psb = ctx.enter_context(nc.psum_tensor("psBt", [128, 1024], BF16))

        p0_load(P, x, g)
        if upto >= 1:
            attn_a(P, x, g)
        if upto >= 2:
            ffn_call(P, x, g, 0)
        if upto >= 3:
            attn_b(P, x, g)
        if upto >= 4:
            ffn_call(P, x, g, 1)
        p5_store(P, x, g)

        with nc.Block() as block:
            em.render(block)
    return nc


def make_in_maps(inputs):
    tbl = host_tables()
    X = np.asarray(inputs["L0_attn_a_x"], np.float32)
    shared = {
        "aWq": np.ascontiguousarray(inputs["L0_attn_a_Wq"], dtype=np.float32),
        "aWk": np.ascontiguousarray(inputs["L0_attn_a_Wk"], dtype=np.float32),
        "aWv": np.ascontiguousarray(inputs["L0_attn_a_Wv"], dtype=np.float32),
        "aWo": np.ascontiguousarray(inputs["L0_attn_a_Wo"], dtype=np.float32),
        "agq": np.ascontiguousarray(np.asarray(inputs["L0_attn_a_gq"],
                                               np.float32).reshape(-1)),
        "agk": np.ascontiguousarray(np.asarray(inputs["L0_attn_a_gk"],
                                               np.float32).reshape(-1)),
        "agv": np.ascontiguousarray(np.asarray(inputs["L0_attn_a_gv"],
                                               np.float32).reshape(-1)),
        "bWq": np.ascontiguousarray(inputs["L0_attn_b_Wq"], dtype=np.float32),
        "bWk": np.ascontiguousarray(inputs["L0_attn_b_Wk"], dtype=np.float32),
        "bWv": np.ascontiguousarray(inputs["L0_attn_b_Wv"], dtype=np.float32),
        "bWo": np.ascontiguousarray(inputs["L0_attn_b_Wo"], dtype=np.float32),
        "lq1": np.ascontiguousarray(inputs["L0_attn_b_lq1"], dtype=np.float32).reshape(1, 32),
        "lk1": np.ascontiguousarray(inputs["L0_attn_b_lk1"], dtype=np.float32).reshape(1, 32),
        "lq2": np.ascontiguousarray(inputs["L0_attn_b_lq2"], dtype=np.float32).reshape(1, 32),
        "lk2": np.ascontiguousarray(inputs["L0_attn_b_lk2"], dtype=np.float32).reshape(1, 32),
        "rms_w": np.ascontiguousarray(inputs["L0_attn_b_rms_w"], dtype=np.float32),
        "fwqkv": np.ascontiguousarray(inputs["L0_ffn_wqkv"], dtype=np.float32),
        "fqs": np.ascontiguousarray(inputs["L0_ffn_q_scale"], dtype=np.float32),
        "fks": np.ascontiguousarray(inputs["L0_ffn_k_scale"], dtype=np.float32),
        "fwo": np.ascontiguousarray(inputs["L0_ffn_wo"], dtype=np.float32),
        "flng": np.ascontiguousarray(inputs["L0_ffn_ln_g"], dtype=np.float32),
        "flnb": np.ascontiguousarray(inputs["L0_ffn_ln_b"], dtype=np.float32),
        "fw1": np.ascontiguousarray(inputs["L0_ffn_w1"], dtype=np.float32),
        "fb1": np.ascontiguousarray(inputs["L0_ffn_b1"], dtype=np.float32),
        "fw2": np.ascontiguousarray(inputs["L0_ffn_w2"], dtype=np.float32),
        "fb2": np.ascontiguousarray(inputs["L0_ffn_b2"], dtype=np.float32),
    }
    shared.update(tbl)
    B = X.shape[0]
    return [dict(shared, x_in=np.ascontiguousarray(X[b])) for b in range(B)]


_NC = {}

def kernel(**inputs):
    if "nc" not in _NC:
        _NC["nc"] = build_nc(upto=4)
    in_maps = make_in_maps(inputs)
    res = run_bass_kernel_spmd(_NC["nc"], in_maps, core_ids=list(range(len(in_maps))))
    return np.stack([r["out"] for r in res.results], axis=0).astype(np.float32)
